# version 28
# baseline (speedup 1.0000x reference)
import numpy as np
import concourse.bass as bass
import concourse.mybir as mybir
from concourse.bass_isa import ReduceOp
from concourse import library_config as _LC

F32 = mybir.dt.float32
BF16 = mybir.dt.bfloat16
AF = mybir.ActivationFunctionType
ALU = mybir.AluOpType
AX = mybir.AxisListType

D = 1024
DCH = D // 128
E2 = 2048
ECH = E2 // 128
S = 16
I = 4096
ICH = I // 128
EPS = 1e-5
C2688 = 2688.0

_REG = {}


def register_custom_ops():
    if _REG:
        return _REG["qlow"], _REG["qhigh"]
    import concourse.dve_ops as dve_ops
    from concourse.dve_op import DveOp
    from concourse.dve_spec import Spec, Src0, Src1, C0, C1, C2, C3, lower

    def _f(x):
        return x.astype(np.float32) if hasattr(x, "astype") else np.float32(x)

    qlow_body = (Src0 >= C0) + (Src0 >= C1) + (Src0 >= C2) + (Src0 >= C3)

    def qlow_ref(in0, in1, s0, s1, imm2):
        t = _f(in0)
        return ((t >= _f(s0)).astype(np.float32) + (t >= _f(s1)).astype(np.float32)
                + (t >= np.float32(imm2)).astype(np.float32)
                + (t >= _f(in1)).astype(np.float32))

    c4 = (Src0 >= C0); c5 = (Src0 >= C1); c6 = (Src0 >= C2)
    u_ = (c4 + c5) + (c6 + c6)
    qhigh_body = Src1 + (u_ + u_)

    def qhigh_ref(in0, in1, s0, s1, imm2):
        t = _f(in0)
        u = ((t >= _f(s0)).astype(np.float32) + (t >= _f(s1)).astype(np.float32)
             + 2.0 * (t >= np.float32(imm2)).astype(np.float32))
        return _f(in1) + 2.0 * u

    out = []
    for name, body, ref in (("ANT_QLOW", qlow_body, qlow_ref),
                            ("ANT_QHIGH", qhigh_body, qhigh_ref)):
        op = next((o for o in dve_ops.OPS if o.name == name), None)
        if op is None:
            spec = Spec(body=body)
            sha = {}
            for ver in ("v3", "v4"):
                try:
                    sha[ver] = lower(spec, ver=ver, perf_en=False, name=name).sha(ver)
                except Exception:
                    pass
            op = DveOp(name, spec, subdim=False, uops_sha=sha, reference=ref)
            dve_ops.OPS.append(op)
            dve_ops._SUB_OPCODE_FOR_NAME[name] = max(dve_ops._SUB_OPCODE_FOR_NAME.values()) + 1
            assert max(dve_ops._SUB_OPCODE_FOR_NAME.values()) < 0x20
            dve_ops.CUSTOM_DVE_REFS[name] = op.ref
            dve_ops.CUSTOM_DVE_SPECS[name] = spec
        out.append(op)
    _REG["qlow"], _REG["qhigh"] = out
    return out


class Sync:
    def __init__(self, nc):
        self.nc = nc
        self.eng = {"pe": nc.tensor, "act": nc.scalar, "dve": nc.vector,
                    "pool": nc.gpsimd, "sp": nc.sync}
        self.sem = {k: nc.alloc_semaphore(f"prog_{k}") for k in self.eng}
        self.cnt = {k: 0 for k in self.eng}
        self.seen = {(a, b): 0 for a in self.eng for b in self.eng}
        self.dq = {"sp": nc.sync, "pool": nc.gpsimd, "act": nc.scalar,
                   "vec": nc.vector, "ten": nc.tensor}
        self.dsem = {}
        self.dcnt = {}
        self.dseen = {}
        self.cc_sem = nc.alloc_semaphore("cc_sem")
        self.cc_cnt = 0

    def tick(self, key, instr):
        self.cnt[key] += 1
        instr.then_inc(self.sem[key], 1)
        return self.cnt[key]

    def wait(self, waiter, producer, upto=None):
        v = self.cnt[producer] if upto is None else upto
        if self.seen[(waiter, producer)] >= v or v <= 0:
            return
        self.eng[waiter].wait_ge(self.sem[producer], v)
        self.seen[(waiter, producer)] = v

    SERIAL_DMA = False

    def _ch(self, ch):
        if self.SERIAL_DMA:
            ch = {"wqst": "pool"}.get(ch, "sp") if ch not in ("sp", "pool", "act") else ch
        if ch not in self.dsem:
            self.dsem[ch] = self.nc.alloc_semaphore(f"dma_{ch}")
            self.dcnt[ch] = 0
            for a in self.eng:
                self.dseen[(a, ch)] = 0
        return ch

    def dma(self, qkey, dst, src, ch=None):
        if self.SERIAL_DMA:
            ch = self._ch(qkey)
            self.dwait(qkey, ch)
        else:
            ch = self._ch(ch or qkey)
        self.dq[qkey].dma_start(dst, src).then_inc(self.dsem[ch], 16)
        self.dcnt[ch] += 16
        return self.dcnt[ch]

    def dmat(self, dst, src, ch):
        ch = self._ch(ch)
        self.nc.sync.dma_start_transpose(dst, src).then_inc(self.dsem[ch], 16)
        self.dcnt[ch] += 16
        return self.dcnt[ch]

    def dwait(self, waiter, ch, upto=None):
        if self.SERIAL_DMA and upto is not None:
            upto = None
        ch = self._ch(ch)
        v = self.dcnt[ch] if upto is None else upto
        if self.dseen[(waiter, ch)] >= v or v <= 0:
            return
        self.eng[waiter].wait_ge(self.dsem[ch], v)
        self.dseen[(waiter, ch)] = v


def build(T=2048, TC=512, phases=("W1", "A", "W2", "B"), n_cores=8):
    QLOW, QHIGH = register_custom_ops()
    NCH = T // TC
    TSUB = TC // 128
    nc = bass.Bass("TRN2", target_bir_lowering=False)
    sy = Sync(nc)
    sy.xsm_rd = {0: (0, 0), 1: (0, 0)}
    sy.wstrip_rd = {0: 0, 1: 0}
    sy.gustrip_rd = {0: 0, 1: 0}
    sy.gups_ev = {0: (0, 0), 1: (0, 0)}
    sy.tr_ev = {0: 0, 1: 0}
    sy.w2_st = 0
    sy.hid_q = {0: 0, 1: 0, 2: 0, 3: 0}
    sy.hq_pe = {0: 0, 1: 0, 2: 0, 3: 0}
    sy.aring_rd = [0] * 8
    sy.dring_rd = [0] * 8
    sy.win_ev = {0: 0, 1: 0}
    sy.hq_pe_all = 0
    sy.hq_ev = (0, 0)
    sy.xsm_out = {0: 0, 1: 0}
    V, Sc, PE, GP = nc.vector, nc.scalar, nc.tensor, nc.gpsimd

    dram = lambda n, shp, dt=F32, kind="ExternalInput": nc.dram_tensor(n, shp, dt, kind=kind).ap()
    x_h = dram("x_h", [T, D])
    winT_h = dram("winT_h", [D, E2])
    wdtT_h = dram("wdtT_h", [D, D])
    wxT_h = dram("wxT_h", [D, 2 * S])
    woutT_h = dram("woutT_h", [D, D])
    convw_h = dram("convw_h", [128, DCH, 3])
    gamma_h = dram("gamma_h", [128, DCH])
    beta_h = dram("beta_h", [128, DCH])
    bdt_h = dram("bdt_h", [128, DCH])
    Dp_h = dram("Dp_h", [128, DCH])
    Alog_h = dram("Alog_h", [128, DCH, S])
    gwsh_h = dram("gwsh_h", [I // 8, D])
    uwsh_h = dram("uwsh_h", [I // 8, D])
    dwsh_h = dram("dwsh_h", [D // 8, I])
    sin_h = dram("sin_h", [128, 1])
    sind_h = dram("sind_h", [128, 1])
    idb_h = dram("idb_h", [128, 128], BF16)
    idf_h = dram("idf_h", [128, 128])
    out_h = dram("out_h", [T, D], kind="ExternalOutput")
    x2_h = dram("x2_h", [T, D], kind="Internal")
    winb_h = dram("winb_h", [ECH, 128, DCH, 128], BF16, kind="Internal")
    wdtb_h = dram("wdtb_h", [DCH, 128, DCH, 128], BF16, kind="Internal")
    woutb_h = dram("woutb_h", [DCH, 2, 128, 512], BF16, kind="Internal")
    wgT_h = dram("wgT_h", [I // 256, 128, DCH, 256], BF16, kind="Internal")
    wuT_h = dram("wuT_h", [I // 256, 128, DCH, 256], BF16, kind="Internal")
    wdT_h = dram("wdT_h", [I, D], BF16, kind="Internal")
    gsin_h = dram("gsin_h", [1, 4], kind="Internal")
    gsout_h = dram("gsout_h", [1, 4], kind="Internal")
    gsh_b = dram("gsh_b", [D, I // 8], BF16, kind="Internal")
    ush_b = dram("ush_b", [D, I // 8], BF16, kind="Internal")
    dsh_b = dram("dsh_b", [I, D // 8], BF16, kind="Internal")
    gg_h = dram("gg_h", [8, D, I // 8], BF16, kind="Internal")
    ug_h = dram("ug_h", [8, D, I // 8], BF16, kind="Internal")
    dg_h = dram("dg_h", [8, I, D // 8], BF16, kind="Internal")

    sb = lambda n, shp, dt=F32: nc.alloc_sbuf_tensor(n, shp, dt).ap()
    convw = sb("convw", [128, DCH, 3])
    gamma = sb("gamma", [128, DCH]); beta = sb("beta", [128, DCH])
    bdt = sb("bdt", [128, DCH]); Dp = sb("Dp", [128, DCH])
    Acol = sb("Acol", [128, DCH, S])
    sin_t = sb("sin_t", [128, 1]); sind_t = sb("sind_t", [128, 1])
    idb = sb("idb", [128, 128], BF16)
    idf = sb("idf", [128, 128])
    selB = sb("selB", [2 * S, 128])
    selC = sb("selC", [2 * S, 128])
    ones32 = sb("ones32", [2 * S, 128])
    be_t = sb("be_t", [128, ECH])
    st_t = sb("st_t", [128, DCH, S])
    halo = sb("halo", [128, DCH, 1])
    stat = sb("stat", [128, 16])
    gs_t = sb("gs_t", [128, 4])
    cEPS = sb("cEPS", [128, 1])
    cONE = sb("cONE", [128, 1])
    dump = sb("dump", [128, 1024])
    arenaA = sb("arenaA", [128, 8192])
    wldf = arenaA[:, 0:2048]
    xxb = arenaA[:, 2048:2048 + DCH * (TC + 2)].rearrange("p (c t) -> p c t", c=DCH)
    _o = 2048 + DCH * (TC + 2)
    dA2 = [arenaA[:, _o:_o + TC], arenaA[:, _o + TC:_o + 2 * TC]]; _o += 2 * TC
    dBb = arenaA[:, _o:_o + TC]; _o += TC
    hsc = sb("hsc", [128, TC])
    p2 = [sb("p2a", [128, TC]), sb("p2b", [128, TC])]
    assert _o <= 8192, _o
    hqT = arenaA.bitcast(BF16).rearrange("p (c t) -> p c t", c=ICH)
    xsm1 = sb("xsm1", [128, D])
    ln1 = sb("ln1", [128, TSUB, D])
    szb = sb("szb", [128, DCH, TC])
    xcb = sb("xcb", [128, DCH, TC])
    dtb = sb("dtb", [128, DCH, TC])
    yac = sb("yac", [128, DCH, TC])
    ln2s = sb("ln2s", [128, TSUB, D])
    hbfb = sb("hbfb", [128, DCH, TC], BF16)
    hbf2 = sb("hbf2", [128, DCH, TC], BF16)
    hfmh = sb("hfmh", [128, DCH, 1], BF16)
    hidb = sb("hidb", [128, I])
    wstg = sb("wstg", [128, DCH, 512], BF16)
    wstg2 = sb("wstg2", [128, DCH, 512], BF16)
    wstg3 = sb("wstg3", [128, 1024], BF16)
    wstg4 = sb("wstg4", [128, 1024], BF16)
    qsml = sb("qsml", [128, 2320])
    bct = qsml[0:2 * S, 0:TC]
    xh1 = qsml[0:1, 1024:2048]
    lh1 = qsml[0:1, 0:1024]
    lh2 = dump[0:1, 0:1024]
    lh2b = yac.rearrange("p c t -> p (c t)").bitcast(BF16)[0:1, 0:1024]
    wxT = sb("wxT", [128, DCH, 2 * S], BF16)

    psa = lambda n, shp: nc.alloc_psum_tensor(n, shp, F32).ap()
    P_a = psa("P_a", [128, 512])
    P_b = psa("P_b", [128, 512])
    P_c = psa("P_c", [128, 512])
    P_d = psa("P_d", [128, 512])
    P_e = psa("P_e", [128, 512])
    P_f = psa("P_f", [128, 512])
    P_g = psa("P_g", [128, 1024])
    P_fb = P_f.bitcast(BF16)
    P_eb = P_e.bitcast(BF16)

    def ln_stats(src_ap, n_feat, mu_col, rs_col):
        sy.wait("act", "dve")
        sy.tick("act", Sc.activation(dump[0:src_ap.shape[0], 0:n_feat], src_ap, AF.Square,
                                     accum_out=stat[0:src_ap.shape[0], 2:3]))
        sy.wait("dve", "dve")
        sy.tick("dve", V.reduce_sum(stat[0:src_ap.shape[0], 0:1], src_ap, AX.X))
        sy.wait("dve", "act")
        sy.wait("dve", "dve")
        sy.tick("dve", V.tensor_scalar(mu_col, stat[0:src_ap.shape[0], 0:1], 1.0 / n_feat, None, ALU.mult))
        sy.tick("dve", V.tensor_scalar(stat[0:src_ap.shape[0], 3:4], stat[0:src_ap.shape[0], 2:3], 1.0 / n_feat, None, ALU.mult))
        sy.wait("dve", "dve")
        sy.tick("dve", V.tensor_tensor(stat[0:src_ap.shape[0], 4:5], mu_col, mu_col, ALU.mult))
        sy.wait("dve", "dve")
        sy.tick("dve", V.tensor_tensor(stat[0:src_ap.shape[0], 5:6], stat[0:src_ap.shape[0], 3:4], stat[0:src_ap.shape[0], 4:5], ALU.subtract))
        sy.wait("act", "dve")
        sy.tick("act", Sc.activation(stat[0:src_ap.shape[0], 6:7], stat[0:src_ap.shape[0], 5:6], AF.Sqrt, bias=cEPS[0:src_ap.shape[0], :]))
        sy.wait("dve", "act")
        sy.tick("dve", V.reciprocal(rs_col, stat[0:src_ap.shape[0], 6:7]))

    def ln_apply(dst_ap, src_ap, mu_col, rs_col):
        sy.wait("dve", "dve")
        sy.tick("dve", V.tensor_scalar(dst_ap, src_ap, mu_col, rs_col, ALU.subtract, ALU.mult))

    MU = lambda p=128: stat[0:p, 7:8]
    RS = lambda p=128: stat[0:p, 8:9]

    def quant_s1(src_ap, nblk, gs_col, sq_off, bufs):
        p = src_ap.shape[0]
        o = sq_off
        amax = qsml[0:p, o:o + nblk]; o += nblk
        sc = qsml[0:p, o:o + nblk]; o += nblk
        b1 = qsml[0:p, o:o + nblk]; o += nblk
        g = qsml[0:p, o:o + nblk]; o += nblk
        r = qsml[0:p, o:o + nblk]; o += nblk
        n1 = qsml[0:p, o:o + nblk]; o += nblk
        n2 = qsml[0:p, o:o + nblk]; o += nblk
        dq = qsml[0:p, o:o + nblk]; o += nblk
        rdq = qsml[0:p, o:o + nblk]; o += nblk
        src3 = src_ap.rearrange("p (j k) -> p j k", k=16)
        a_s = bufs[0][0:p, 0:nblk * 16]
        s_s = bufs[1][0:p, 0:nblk * 16]
        n_s = bufs[2][0:p, 0:nblk * 16]
        q2s = bufs[3][0:p, 0:nblk * 16]
        sy.wait("act", "dve")
        sy.wait("act", "pool", quant.out_t)
        sgn_t = sy.tick("act", Sc.sign(n_s, src_ap))
        sy.wait("dve", "dve")
        sy.tick("dve", V.tensor_reduce(amax, src3, AX.X, ALU.max, apply_absolute_value=True))
        sy.wait("dve", "dve")
        sy.tick("dve", V.tensor_scalar(stat[0:p, 9:10], gs_col, 1.0 / 6.0, None, ALU.mult))
        sy.wait("dve", "dve")
        sy.tick("dve", V.tensor_scalar(sc, amax, stat[0:p, 9:10], 448.0, ALU.mult, ALU.min))
        sy.wait("dve", "dve")
        sy.tick("dve", V.tensor_scalar(b1, sc, 8.0, 1.0, ALU.is_ge, ALU.add))
        for thr in (16.0, 32.0, 64.0, 128.0, 256.0):
            sy.wait("dve", "dve")
            sy.tick("dve", V.tensor_scalar(g, sc, thr, 1.0, ALU.is_ge, ALU.add))
            sy.wait("dve", "dve")
            sy.tick("dve", V.tensor_tensor(b1, b1, g, ALU.mult))
        sy.wait("dve", "dve")
        sy.tick("dve", V.tensor_scalar(g, b1, 0.5, None, ALU.mult))
        sy.wait("dve", "dve")
        sy.tick("dve", V.reciprocal(n2, g))
        sy.wait("dve", "dve")
        sy.tick("dve", V.tensor_tensor(r, sc, n2, ALU.mult))
        sy.wait("dve", "dve")
        sy.tick("dve", V.tensor_scalar(n1, r, 8.5, None, ALU.is_ge))
        for j in range(1, 8):
            sy.wait("dve", "dve")
            sy.tick("dve", V.scalar_tensor_tensor(n1, r, 8.5 + j, n1, ALU.is_ge, ALU.add))
        sy.wait("dve", "dve")
        sy.tick("dve", V.scalar_tensor_tensor(sc, n1, 8.0, g, ALU.add, ALU.mult))
        sy.wait("dve", "dve")
        sy.tick("dve", V.reciprocal(stat[0:p, 15:16], gs_col))
        sy.wait("dve", "dve")
        sy.tick("dve", V.tensor_scalar(dq, sc, stat[0:p, 15:16], None, ALU.mult))
        sy.wait("dve", "dve")
        sy.tick("dve", V.reciprocal(rdq, dq))
        sy.tick("dve", V.tensor_scalar(dq, dq, 0.5, None, ALU.mult))
        sy.wait("dve", "dve")
        rdq_b = rdq.rearrange("p (j o) -> p j o", o=1).broadcast_to([p, nblk, 16])
        dq_b = dq.rearrange("p (j o) -> p j o", o=1).broadcast_to([p, nblk, 16])
        sy.wait("pool", "dve")
        sy.wait("pool", "act")
        sy.wait("pool", "pool")
        sy.tick("pool", GP.tensor_tensor(a_s.rearrange("p (j k) -> p j k", k=16), src3, rdq_b, ALU.mult))
        sy.wait("pool", "pool")
        gsq = sy.tick("pool", GP.tensor_tensor(s_s, a_s, a_s, ALU.mult))
        sy.wait("pool", "act", sgn_t)
        sy.wait("pool", "pool")
        gdqs = sy.tick("pool", GP.tensor_tensor(a_s.rearrange("p (j k) -> p j k", k=16),
                                                n_s.rearrange("p (j k) -> p j k", k=16), dq_b, ALU.mult))
        return dict(a_s=a_s, s_s=s_s, n_s=n_s, q2s=q2s, gdqs=gdqs)

    def quant_s2(st, dst_bf_ap):
        s_s, n_s, q2s, a_s = st["s_s"], st["n_s"], st["q2s"], st["a_s"]
        sy.wait("dve", "pool", st["gdqs"])
        sy.tick("dve", V.tensor_scalar(n_s, s_s, 0.0625, None, ALU.is_ge))
        for thr in (0.5625, 1.5625, 3.0625):
            sy.wait("dve", "dve")
            sy.tick("dve", V.scalar_tensor_tensor(n_s, s_s, thr, n_s, ALU.is_ge, ALU.add))
        sy.wait("dve", "dve")
        sy.tick("dve", V.tensor_scalar(q2s, s_s, 25.0, 2.0, ALU.is_ge, ALU.mult))
        sy.wait("dve", "dve")
        sy.tick("dve", V.scalar_tensor_tensor(q2s, s_s, 6.25, q2s, ALU.is_ge, ALU.add))
        sy.wait("dve", "dve")
        sy.tick("dve", V.scalar_tensor_tensor(q2s, s_s, 12.25, q2s, ALU.is_ge, ALU.add))
        sy.wait("dve", "dve")
        sy.tick("dve", V.scalar_tensor_tensor(q2s, q2s, 2.0, n_s, ALU.mult, ALU.add))
        sy.wait("pool", "dve")
        sy.wait("pool", "pool")
        quant.out_t = sy.tick("pool", GP.tensor_tensor(dst_bf_ap, q2s, a_s, ALU.mult))
        return quant.out_t

    def quant(dst_bf_ap, src_ap, nblk, gs_col, sq_off):
        quant_s2(quant_s1(src_ap, nblk, gs_col, sq_off,
                          (quant.a_s, quant.s_s, quant.n_s, quant.q2s)), dst_bf_ap)

    quant.out_t = 0
    for t, h in ((convw, convw_h), (gamma, gamma_h), (beta, beta_h), (bdt, bdt_h),
                 (Dp, Dp_h), (Acol, Alog_h), (sin_t, sin_h), (sind_t, sind_h),
                 (idb, idb_h), (idf, idf_h)):
        sy.dma("sp", t[:], h[:])
    sy.dwait("act", "sp")
    sy.tick("act", Sc.activation(dump[:, 0:DCH * S], Acol.rearrange("p c s -> p (c s)"), AF.Exp))
    sy.wait("dve", "act")
    sy.tick("dve", V.tensor_scalar(Acol.rearrange("p c s -> p (c s)"), dump[:, 0:DCH * S], -1.0, None, ALU.mult))
    sy.tick("dve", V.memset(st_t[:], 0.0))
    sy.tick("dve", V.memset(halo[:], 0.0))
    sy.tick("dve", V.memset(ones32[:], 1.0))
    sy.tick("dve", V.memset(cEPS[:], EPS))
    sy.tick("dve", V.memset(cONE[:], 1.0))

    do_W1 = "W1" in phases
    wbf = wstg.rearrange("p c t -> p (c t)")
    for dch in (range(DCH) if do_W1 else []):
        sy.wait("sp", "dve")
        sy.wait("sp", "pe")
        sy.dwait("sp", "sp")
        n = sy.dma("sp", wldf[:, 0:E2], winT_h[dch * 128:(dch + 1) * 128, :])
        sy.dwait("dve", "sp", n)
        sy.dwait("dve", "wst")
        sy.tick("dve", V.tensor_scalar(wbf[:, 0:E2], wldf[:, 0:E2], gamma[:, dch:dch + 1], None, ALU.mult))
        sy.dwait("pe", "sp", n)
        for eblk in range(4):
            P_mm = (P_a, P_b, P_c, P_d)[eblk]
            sy.tick("pe", PE.matmul(P_mm[0:1, 0:512], beta[:, dch:dch + 1],
                                    wldf[:, eblk * 512:(eblk + 1) * 512],
                                    start=(dch == 0), stop=(dch == DCH - 1)))
        sy.wait("act", "dve")
        sy.dma("act", winb_h[:, :, dch, :].rearrange("e p k -> p e k"),
               wbf[:, 0:E2].bitcast(BF16).rearrange("p (e k) -> p e k", k=128), ch="wst")
    sy.wait("dve", "pe")
    for eblk in (range(4) if do_W1 else []):
        P_mm = (P_a, P_b, P_c, P_d)[eblk]
        sy.tick("dve", V.tensor_copy(dump[0:1, 0:512], P_mm[0:1, 0:512]))
        sy.wait("pe", "dve")
        for j in range(4):
            ech = eblk * 4 + j
            sy.tick("pe", PE.transpose(P_e[:, ech:ech + 1], dump[0:1, j * 128:(j + 1) * 128],
                                       idf[0:1, 0:1]))
        sy.wait("dve", "pe")
    sy.tick("dve", V.tensor_copy(be_t[:, 0:ECH], P_e[:, 0:ECH]))
    for wi, (src_h, dst_h) in enumerate((((wdtT_h, wdtb_h), (woutT_h, woutb_h)) if do_W1 else ())):
        for dch in range(DCH):
            sy.wait("sp", "dve")
            sy.dwait("sp", "sp")
            n = sy.dma("sp", wldf[:, 0:D], src_h[dch * 128:(dch + 1) * 128, :])
            sy.dwait("dve", "sp", n)
            sy.dwait("dve", "wst")
            sy.tick("dve", V.tensor_copy(wbf[:, 0:D], wldf[:, 0:D]))
            sy.wait("act", "dve")
            if wi == 0:
                sy.dma("act", wdtb_h[:, :, dch, :].rearrange("e p k -> p e k"),
                       wbf[:, 0:D].bitcast(BF16).rearrange("p (e k) -> p e k", k=128), ch="wst")
            else:
                sy.dma("act", woutb_h[dch].rearrange("h p k -> p h k"),
                       wbf[:, 0:D].bitcast(BF16).rearrange("p (h k) -> p h k", k=512), ch="wst")
    for dch in (range(DCH) if do_W1 else []):
        sy.wait("sp", "dve")
        sy.dwait("sp", "sp")
        n = sy.dma("sp", wldf[:, 0:2 * S], wxT_h[dch * 128:(dch + 1) * 128, :])
        sy.dwait("dve", "sp", n)
        sy.tick("dve", V.tensor_copy(wxT[:, dch, :], wldf[:, 0:2 * S]))

    def wq_pass1(w_h, rows, cols, midx):
        nhalf0 = 1 if cols <= 2048 else 2
        csz0 = cols // nhalf0
        nsl = rows // 128
        wbufs0 = [wldf, xcb.rearrange("p c t -> p (c t)")[:, 0:2048]]
        k = 0
        for sl in range(nsl):
            for hf in range(nhalf0):
                par = k % 2
                chw = f"wq{par}"
                sy.wait("sp", "dve", sy.w2rd[par])
                sy.dwait("sp", chw)
                n_w = sy.dma("sp", wbufs0[par][:, 0:csz0],
                             w_h[sl * 128:(sl + 1) * 128, hf * csz0:(hf + 1) * csz0], ch=chw)
                sy.dwait("dve", chw, n_w)
                sy.wait("dve", "dve")
                sy.tick("dve", V.tensor_reduce(qsml[:, k:k + 1], wbufs0[par][:, 0:csz0], AX.X, ALU.max,
                                               apply_absolute_value=True))
                sy.w2rd[par] = sy.cnt["dve"]
                k += 1
        sy.wait("dve", "dve")
        sy.tick("dve", V.tensor_reduce(stat[:, 12 + midx:13 + midx], qsml[:, 0:k], AX.X, ALU.max))

    def wq_pass2(w_h, wq_h, rows, cols, gs_col):
        import os as _os
        nhalf = 1 if cols <= 2048 else 2
        csz = cols // nhalf
        nsl = rows // 128
        wbufs = [wldf, xcb.rearrange("p c t -> p (c t)")[:, 0:2048]]
        k = 0
        for sl in range(nsl):
            for hf in range(nhalf):
                par = k % 2
                wsrc = wbufs[par]
                chw = f"wq{par}"
                sy.wait("sp", "dve", sy.w2rd[par])
                sy.wait("sp", "act", sy.w2rda[par])
                sy.dwait("sp", chw)
                n_w = sy.dma("sp", wsrc[:, 0:csz], w_h[sl * 128:(sl + 1) * 128, hf * csz:(hf + 1) * csz], ch=chw)
                sy.dwait("dve", chw, n_w)
                sy.dwait("act", chw, n_w)
                wqb = wstg.rearrange("p c t -> p (c t)")[:, 0:csz]
                sy.wait("dve", "pe")
                quant(wqb, wsrc[:, 0:csz], csz // 16, gs_col, 0)
                sy.w2rd[par] = sy.cnt["dve"]
                sy.w2rda[par] = sy.cnt["act"]
                q_t = sy.cnt["dve"]
                nb = csz // 128
                sy.wait("pe", "dve", q_t)
                sy.wait("pe", "pool", quant.out_t)
                sy.dwait("act", "wqst", sy.w2_st)
                for bi in range(nb):
                    PB = P_fb if bi % 2 == 0 else P_eb
                    sy.wait("pe", "act", sy.tr_ev[bi % 2])
                    sy.tick("pe", PE.transpose(PB[:, 0:128],
                                               wqb[:, bi * 128:(bi + 1) * 128], idb[:]))
                    sy.wait("act", "pe")
                    sy.tick("act", Sc.activation(wstg2.rearrange("p c t -> p (c t)")[:, bi * 128:(bi + 1) * 128],
                                                 PB[:, 0:128], AF.Copy))
                    sy.tr_ev[bi % 2] = sy.cnt["act"]
                sy.wait("pool", "act")
                sy.w2_st = sy.dma("pool", wq_h[hf * csz:(hf + 1) * csz, sl * 128:(sl + 1) * 128]
                                  .rearrange("(b q) r -> q b r", q=128),
                                  wstg2.rearrange("p c t -> p (c t)")[:, 0:csz].rearrange("q (b r) -> q b r", r=128),
                                  ch="wqst")
                k += 1

    if "W2" in phases or "W2a" in phases:
        for e in ("dve", "act", "pe", "pool"):
            sy.dwait(e, "wst")
            sy.dwait(e, "sp")
        sy.w2rd = {0: sy.cnt["dve"], 1: sy.cnt["dve"]}
        sy.w2rda = {0: sy.cnt["act"], 1: sy.cnt["act"]}
        wq_pass1(gwsh_h, I // 8, D, 0)
        wq_pass1(uwsh_h, I // 8, D, 1)
        wq_pass1(dwsh_h, D // 8, I, 2)
        sy.wait("pe", "dve")
        sy.tick("pe", PE.transpose(P_g[0:1, 0:128], stat[:, 12:13], idf[:]))
        sy.tick("pe", PE.transpose(P_g[0:1, 128:256], stat[:, 13:14], idf[:]))
        sy.tick("pe", PE.transpose(P_g[0:1, 256:384], stat[:, 14:15], idf[:]))
        sy.wait("dve", "pe")
        for m in range(3):
            sy.tick("dve", V.tensor_reduce(dump[0:1, 200 + m:201 + m], P_g[0:1, m * 128:(m + 1) * 128], AX.X, ALU.max))
        sy.wait("dve", "dve")
        sy.tick("dve", V.memset(dump[0:1, 203:204], 1.0))
        sy.wait("pool", "dve")
        sy.dma("pool", gsin_h[:], dump[0:1, 200:204], ch="ccio")
        sy.dwait("pool", "ccio")
        if "B" in phases:
            GP.collective_compute("AllReduce", ALU.max, replica_groups=[list(range(n_cores))],
                                  ins=[gsin_h[:].opt()], outs=[gsout_h[:].opt()]).then_inc(sy.cc_sem, 1)
            sy.cc_cnt += 1
            sy.eng["pool"].wait_ge(sy.cc_sem, sy.cc_cnt)
        else:
            sy.dma("pool", gsout_h[:], gsin_h[:], ch="ccio")
        n_gs = sy.dma("pool", dump[0:1, 208:212], gsout_h[:], ch="ccio")
        sy.dwait("pe", "ccio", n_gs)
        sy.tick("pe", PE.matmul(P_g[:, 0:4], ones32[0:1, :], dump[0:1, 208:212], start=True, stop=True))
        sy.wait("dve", "pe")
        sy.tick("dve", V.reciprocal(stat[:, 8:9], P_g[:, 0:1]))
        sy.tick("dve", V.reciprocal(stat[:, 9:10], P_g[:, 1:2]))
        sy.tick("dve", V.reciprocal(stat[:, 10:11], P_g[:, 2:3]))
        sy.wait("dve", "dve")
        sy.tick("dve", V.tensor_scalar(gs_t[:, 0:1], stat[:, 8:9], C2688, None, ALU.mult))
        sy.tick("dve", V.tensor_scalar(gs_t[:, 1:2], stat[:, 9:10], C2688, None, ALU.mult))
        sy.tick("dve", V.tensor_scalar(gs_t[:, 2:3], stat[:, 10:11], C2688, None, ALU.mult))
        sy.wait("dve", "dve")
    if "W2" in phases:
        quant.a_s = hidb
        quant.s_s = szb.rearrange("p c t -> p (c t)")
        quant.n_s = ln1.rearrange("p u d -> p (u d)")
        quant.q2s = ln2s.rearrange("p u d -> p (u d)")
        wq_pass2(gwsh_h, gsh_b, I // 8, D, gs_t[:, 0:1])
        wq_pass2(uwsh_h, ush_b, I // 8, D, gs_t[:, 1:2])
        wq_pass2(dwsh_h, dsh_b, D // 8, I, gs_t[:, 2:3])
        sy.dwait("pool", "wqst")
        if "B" in phases:
            for src, dst in ((gsh_b, gg_h), (ush_b, ug_h), (dsh_b, dg_h)):
                GP.collective_compute("AllGather", ALU.bypass, replica_groups=[list(range(n_cores))],
                                      ins=[src[:].opt()], outs=[dst[:].opt()]).then_inc(sy.cc_sem, 1)
                sy.cc_cnt += 1

    sy.cc_issued = False

    def cc_issue():
        if sy.cc_issued or not ("W2" in phases and "B" in phases):
            return
        sy.cc_issued = True
        sy.eng["sp"].wait_ge(sy.cc_sem, sy.cc_cnt)
        sy.eng["act"].wait_ge(sy.cc_sem, sy.cc_cnt)
        for sh in range(8):
            for half in range(2):
                sy.dma("sp", wgT_h[2 * sh + half],
                       gg_h[sh][:, half * 256:(half + 1) * 256]
                       .rearrange("(c p) e -> p c e", p=128), ch="rp")
                sy.dma("act", wuT_h[2 * sh + half],
                       ug_h[sh][:, half * 256:(half + 1) * 256]
                       .rearrange("(c p) e -> p c e", p=128), ch="rp")
            sy.dma("sp", wdT_h[:, sh * (D // 8):(sh + 1) * (D // 8)], dg_h[sh], ch="rp")

    def cc_finish():
        cc_issue()
        if "W2" in phases and "B" in phases:
            sy.dwait("sp", "rp")
            sy.dwait("act", "rp")
            sy.dwait("pool", "rp")


    sy.dwait("sp", "wst")
    if "W2" in phases:
        for e in ("sp", "act", "dve", "pe"):
            sy.wait(e, "pool")
            sy.dwait(e, "wq0")
            sy.dwait(e, "wq1")
            sy.dwait(e, "wqst")
    if "A" in phases:
        sy.wait("sp", "pool")
        sy.wait("act", "pool")
        for dch in range(DCH):
            sy.n_wout0 = sy.dma("sp", wstg[:, dch, :], woutb_h[dch, 0], ch="wout0")
            sy.n_wout1 = sy.dma("act", wstg2[:, dch, :], woutb_h[dch, 1], ch="wout1")
    lastxx = sy.cnt["dve"]
    for c in (range(NCH) if "A" in phases else []):
        t0 = c * TC
        if c == 2:
            cc_issue()
        sy.dwait("sp", "pool")
        sy.wait("sp", "dve")
        has_halo = (t0 + TC) < T
        if has_halo:
            n_xh = sy.dma("sp", xh1[:], x_h[t0 + TC:t0 + TC + 1, :], ch="xh1")
        for ts in range(TSUB):
            xsm = xsm1 if ts % 2 == 0 else hidb[:, 0:D]
            chx = f"xsm{ts % 2}"
            sy.wait("sp", "act", sy.xsm_rd[ts % 2][0])
            sy.wait("sp", "dve", sy.xsm_rd[ts % 2][1])
            n_x = sy.dma("sp", xsm, x_h[t0 + ts * 128:t0 + (ts + 1) * 128, :], ch=chx)
            sy.dwait("act", chx, n_x)
            sy.dwait("dve", chx, n_x)
            ln_stats(xsm, D, MU(), RS())
            ln_apply(ln1[:, ts, :], xsm, MU(), RS())
            sy.xsm_rd[ts % 2] = (sy.cnt["act"], sy.cnt["dve"])
            ln_stats(ln1[:, ts, :], D, MU(), RS())
            ln_apply(ln2s[:, ts, :], ln1[:, ts, :], MU(), RS())
        if has_halo:
            sy.dwait("act", "xh1", n_xh)
            sy.dwait("dve", "xh1", n_xh)
            ln_stats(xh1[:], D, MU(1), RS(1))
            ln_apply(lh1[:], xh1[:], MU(1), RS(1))
            ln_stats(lh1[:], D, MU(1), RS(1))
            ln_apply(lh2[:], lh1[:], MU(1), RS(1))
            sy.wait("dve", "dve")
            sy.tick("dve", V.tensor_copy(lh2b[:], lh2[:]))
        ln2bf = hbf2.rearrange("p c t -> p (c t)")
        sy.wait("dve", "dve")
        sy.tick("dve", V.tensor_copy(ln2bf[:, :], ln2s.rearrange("p u d -> p (u d)")))
        ln2bf4 = hbf2.rearrange("p c t -> p (c t)").rearrange("p (u d) -> p u d", u=TSUB)
        for dch in range(DCH):
            sy.wait("pe", "dve")
            sy.wait("pe", "act")
            for ts in range(TSUB):
                sy.tick("pe", PE.transpose(P_fb[:, ts * 128:(ts + 1) * 128],
                                           ln2bf4[:, ts, dch * 128:(dch + 1) * 128], idb[:]))
            if has_halo:
                sy.tick("pe", PE.transpose(P_eb[:, 0:1], lh2b[0:1, dch * 128:(dch + 1) * 128], idb[0:1, 0:1]))
            sy.wait("act", "pe")
            sy.tick("act", Sc.activation(hbfb[:, dch, :], P_fb[:, 0:TC], AF.Copy))
            if has_halo:
                sy.tick("act", Sc.activation(hfmh[:, dch, :], P_eb[:, 0:1], AF.Copy))
        hidb_bfA = hidb.bitcast(BF16)
        A_slots = [wstg3.rearrange("p (c e) -> p c e", c=DCH),
                   wstg4.rearrange("p (c e) -> p c e", c=DCH)] + [
            hidb_bfA[:, 2048 + k * 1024:2048 + (k + 1) * 1024]
            .rearrange("p (c e) -> p c e", c=DCH) for k in range(6)]
        sy.wait("pe", "act")
        sy.wait("dve", "act")
        sy.wait("dve", "dve")
        sy.tick("dve", V.tensor_copy(xxb[:, :, 0:1], halo[:]))
        for ech in range(ECH):
            sl = ech % 8
            qw = ("sp", "act", "pool")[sl % 3]
            chw = (f"wsr{sl}" if qw != "pool" else f"pwsr{sl}")
            sy.wait(qw, "pe", sy.aring_rd[sl])
            stg4 = A_slots[sl]
            n_w = sy.dma(qw, stg4, winb_h[ech], ch=chw)
            sy.dwait("pe", chw, n_w)
            P_mm = P_a if ech % 2 == 0 else P_b
            sy.wait("pe", "act", sy.win_ev[ech % 2])
            for dch in range(DCH):
                sy.tick("pe", PE.matmul(P_mm[:, 0:TC], stg4[:, dch, :], hbfb[:, dch, :],
                                        start=(dch == 0), stop=(dch == DCH - 1)))
            if ech < DCH and has_halo:
                for dch in range(DCH):
                    sy.tick("pe", PE.matmul(P_c[:, ech:ech + 1], stg4[:, dch, :], hfmh[:, dch, :],
                                            start=(dch == 0), stop=(dch == DCH - 1)))
            sy.aring_rd[sl] = sy.cnt["pe"]
            sy.wait("act", "pe")
            if ech < DCH:
                sy.wait("act", "dve", lastxx)
                sy.tick("act", Sc.activation(xxb[:, ech, 1:TC + 1], P_mm[:, 0:TC], AF.Identity,
                                             bias=be_t[:, ech:ech + 1]))
            else:
                sy.tick("act", Sc.activation(szb[:, ech - DCH, :], P_mm[:, 0:TC], AF.Silu,
                                             bias=be_t[:, ech:ech + 1]))
            sy.win_ev[ech % 2] = sy.cnt["act"]
            if ech < DCH:
                sy.wait("dve", "act")
                sy.wait("dve", "dve")
                sy.tick("dve", V.tensor_scalar(xcb[:, ech, :], xxb[:, ech, 0:TC], convw[:, ech, 0:1], None, ALU.mult))
                sy.wait("dve", "dve")
                sy.tick("dve", V.scalar_tensor_tensor(xcb[:, ech, :], xxb[:, ech, 1:TC + 1], convw[:, ech, 1:2], xcb[:, ech, :], ALU.mult, ALU.add))
        if has_halo:
            sy.wait("act", "pe")
            for ech in range(DCH):
                sy.tick("act", Sc.activation(xxb[:, ech, TC + 1:TC + 2], P_c[:, ech:ech + 1], AF.Identity,
                                             bias=be_t[:, ech:ech + 1]))
        else:
            sy.wait("dve", "dve")
            sy.tick("dve", V.memset(xxb[:, :, TC + 1:TC + 2], 0.0))
        for dch in range(DCH):
            sy.wait("dve", "act")
            sy.wait("dve", "dve")
            sy.tick("dve", V.scalar_tensor_tensor(xcb[:, dch, :], xxb[:, dch, 2:TC + 2], convw[:, dch, 2:3], xcb[:, dch, :], ALU.mult, ALU.add))
        sy.wait("dve", "dve")
        sy.tick("dve", V.tensor_copy(halo[:], xxb[:, :, TC:TC + 1]))
        sy.wait("act", "dve")
        sy.wait("act", "pe")
        sy.tick("act", Sc.activation(xcb.rearrange("p c t -> p (c t)"), xcb.rearrange("p c t -> p (c t)"), AF.Silu))
        sy.wait("act", "act")
        sy.tick("act", Sc.activation(hbf2.rearrange("p c t -> p (c t)"), xcb.rearrange("p c t -> p (c t)"), AF.Copy))
        xcbf = hbf2
        sy.win_ev = {0: sy.cnt["act"], 1: sy.cnt["act"]}
        for ech in range(DCH):
            P_mm = P_a if ech % 2 == 0 else P_b
            sl = ech % 8
            qw = ("sp", "act", "pool")[sl % 3]
            chw = (f"wsr{sl}" if qw != "pool" else f"pwsr{sl}")
            sy.wait(qw, "pe", sy.aring_rd[sl])
            stg4 = A_slots[sl]
            n_w = sy.dma(qw, stg4, wdtb_h[ech], ch=chw)
            sy.dwait("pe", chw, n_w)
            sy.wait("pe", "act", sy.win_ev[ech % 2])
            for dch in range(DCH):
                sy.tick("pe", PE.matmul(P_mm[:, 0:TC], stg4[:, dch, :],
                                        xcbf[:, dch, :], start=(dch == 0), stop=(dch == DCH - 1)))
            sy.aring_rd[sl] = sy.cnt["pe"]
            sy.wait("act", "pe")
            sy.wait("act", "act")
            sy.tick("act", Sc.activation(dump[:, 0:TC], P_mm[:, 0:TC], AF.Exp, bias=bdt[:, ech:ech + 1]))
            sy.win_ev[ech % 2] = sy.cnt["act"]
            sy.wait("act", "act")
            sy.wait("act", "dve")
            sy.tick("act", Sc.activation(dtb[:, ech, :], dump[:, 0:TC], AF.Ln, bias=cONE[:]))
        sy.wait("pe", "dve")
        for dch in range(DCH):
            sy.tick("pe", PE.matmul(P_d[0:2 * S, 0:TC], wxT[:, dch, :], xcbf[:, dch, :],
                                    start=(dch == 0), stop=(dch == DCH - 1)))
        sy.wait("dve", "pe")
        sy.wait("dve", "dve")
        sy.tick("dve", V.tensor_copy(bct[:], P_d[0:2 * S, 0:TC]))
        scan_exp_t = {}
        scan_rd_t = {}
        pmul_t = {}
        gp_acc_t = {}
        bc_t = {}
        def emit_exp(s_, dch_):
            slot = (s_ * DCH + dch_) % 2
            prev = (s_ * DCH + dch_) - 2
            if prev >= 0:
                sy.wait("act", "dve", scan_rd_t.get((prev // DCH, prev % DCH), 0))
            scan_exp_t[(s_, dch_)] = sy.tick("act", Sc.activation(
                dA2[slot], dtb[:, dch_, :], AF.Exp, scale=Acol[:, dch_, s_:s_ + 1]))
        PBC = ((P_e, P_f), (P_g[:, 0:512], P_g[:, 512:1024]))
        slB = (dump[:, 0:TC], dump[:, TC:2 * TC])
        slC = (xsm1[:, 0:TC], xsm1[:, TC:2 * TC])
        dBp = (hidb[:, 0:TC], hidb[:, TC:2 * TC])
        bcev = {}
        dbt = {}

        def emit_bc(s_):
            if s_ == 0:
                sy.wait("pool", "dve", 10)
            sy.wait("pool", "pe", bc_t.get(s_ - 1, 0))
            sy.tick("pool", GP.affine_select(selB[:], ones32[:], [[0, 128]], ALU.is_equal, 0.0,
                                             base=-s_, channel_multiplier=1))
            sy.tick("pool", GP.affine_select(selC[:], ones32[:], [[0, 128]], ALU.is_equal, 0.0,
                                             base=-(S + s_), channel_multiplier=1))
            sy.wait("pe", "pool")
            if s_ > 1:
                sy.wait("pe", "act", bcev[s_ - 2])
            pbB, pbC = PBC[s_ % 2]
            sy.tick("pe", PE.matmul(pbB[:, 0:TC], selB[:], bct[:], start=True, stop=True))
            bc_t[s_] = sy.tick("pe", PE.matmul(pbC[:, 0:TC], selC[:], bct[:], start=True, stop=True))
            sy.wait("act", "pe")
            sy.wait("act", "act")
            if s_ > 1:
                sy.wait("act", "pool", dbt.get((s_ - 2, DCH - 1), 0))
                sy.wait("act", "dve", pmul_t.get((s_ - 2, DCH - 1), 0))
            sy.tick("act", Sc.activation(slB[s_ % 2], pbB[:, 0:TC], AF.Copy))
            bcev[s_] = sy.tick("act", Sc.activation(slC[s_ % 2], pbC[:, 0:TC], AF.Copy))

        def emit_dB(k_):
            s_, dch_ = k_ // DCH, k_ % DCH
            sy.wait("pool", "act", bcev[s_])
            kp = k_ - 2
            if kp >= 0:
                sy.wait("pool", "dve", scan_rd_t.get((kp // DCH, kp % DCH), 0))
            sy.wait("pool", "pool")
            dbt[(s_, dch_)] = sy.tick("pool", GP.tensor_tensor(
                dBp[k_ % 2], dtb[:, dch_, :], slB[s_ % 2], ALU.mult))
        bct_done = sy.cnt["dve"]
        sy.wait("pe", "dve", bct_done)
        emit_bc(0)
        emit_exp(0, 0)
        emit_dB(0)
        for s in range(S):
            if s + 1 < S:
                emit_bc(s + 1)
            for dch in range(DCH):
                k = s * DCH + dch
                slot = k % 2
                pslot = k % 2
                if k + 1 < S * DCH:
                    emit_exp((k + 1) // DCH, (k + 1) % DCH)
                    emit_dB(k + 1)
                sy.wait("dve", "act", scan_exp_t[(s, dch)])
                sy.wait("dve", "pool", dbt[(s, dch)])
                kprev = k - 2
                if kprev >= 0:
                    sy.wait("dve", "pool", gp_acc_t.get((kprev // DCH, kprev % DCH), 0))
                if dch == 0:
                    sy.wait("dve", "act", bcev[s])
                sy.wait("dve", "dve")
                sy.tick("dve", V.tensor_tensor_scan(hsc[:], dA2[slot], dBp[k % 2], st_t[:, dch, s:s + 1], ALU.mult, ALU.add))
                sy.wait("dve", "dve")
                scan_rd_t[(s, dch)] = sy.cnt["dve"]
                sy.tick("dve", V.tensor_copy(st_t[:, dch, s:s + 1], hsc[:, TC - 1:TC]))
                pmul_t[(s, dch)] = sy.tick("dve", V.tensor_tensor(p2[pslot], hsc[:], slC[s % 2], ALU.mult))
                sy.wait("pool", "dve", pmul_t[(s, dch)])
                if s == 0:
                    gp_acc_t[(s, dch)] = sy.tick("pool", GP.tensor_copy(yac[:, dch, :], p2[pslot]))
                else:
                    gp_acc_t[(s, dch)] = sy.tick("pool", GP.tensor_tensor(yac[:, dch, :], yac[:, dch, :], p2[pslot], ALU.add))
        sy.wait("dve", "pool")
        for dch in range(DCH):
            sy.wait("dve", "dve")
            sy.tick("dve", V.scalar_tensor_tensor(yac[:, dch, :], xcb[:, dch, :], Dp[:, dch:dch + 1], yac[:, dch, :], ALU.mult, ALU.add))
            sy.wait("dve", "dve")
            sy.wait("dve", "pe")
            sy.tick("dve", V.tensor_tensor(hbfb[:, dch, :], yac[:, dch, :], szb[:, dch, :], ALU.mult))
        ybf = hbfb
        P4 = (P_a, P_b, P_c, P_d)
        if c == 0:
            sy.dwait("pe", "wout0", sy.n_wout0)
            sy.dwait("pe", "wout1", sy.n_wout1)
        for half in range(2):
            wres = wstg if half == 0 else wstg2
            for dch in range(DCH):
                if dch == 0:
                    sy.wait("pe", "dve")
                    sy.wait("pe", "act")
                for ts in range(TSUB):
                    sy.tick("pe", PE.matmul(P4[ts][:, 0:512], ybf[:, dch, ts * 128:(ts + 1) * 128],
                                            wres[:, dch, :], start=(dch == 0), stop=(dch == DCH - 1)))
            sy.wait("dve", "pe")
            sy.wait("dve", "act")
            for ts in range(TSUB):
                sy.tick("dve", V.tensor_tensor(ln2s[:, ts, half * 512:(half + 1) * 512],
                                               P4[ts][:, 0:512],
                                               ln1[:, ts, half * 512:(half + 1) * 512], ALU.add))
        sy.xsm_rd = {0: (sy.cnt["act"], sy.cnt["dve"]), 1: (sy.cnt["act"], sy.cnt["dve"])}
        sy.wait("sp", "pool")
        sy.wait("act", "pool")
        for ts in range(TSUB):
            xsm = xsm1 if ts % 2 == 0 else hidb[:, 0:D]
            chx = f"xsm{ts % 2}"
            sy.wait("sp", "act", sy.xsm_rd[ts % 2][0])
            sy.wait("sp", "dve", sy.xsm_rd[ts % 2][1])
            n_x = sy.dma("sp", xsm, x_h[t0 + ts * 128:t0 + (ts + 1) * 128, :], ch=chx)
            ln_stats(ln2s[:, ts, :], D, MU(), RS())
            ln_apply(dump[:, 0:D], ln2s[:, ts, :], MU(), RS())
            sy.dwait("dve", chx, n_x)
            sy.wait("dve", "dve")
            sy.tick("dve", V.tensor_tensor(ln1[:, ts, :], xsm, dump[:, 0:D], ALU.add))
            sy.xsm_rd[ts % 2] = (sy.cnt["act"], sy.cnt["dve"])
        sy.wait("pool", "dve")
        sy.dma("pool", x2_h[t0:t0 + TC, :].rearrange("(u p) d -> p u d", p=128), ln1[:])
        lastxx = sy.cnt["dve"]

    for e in ("dve", "act", "pe", "pool"):
        sy.dwait(e, "pool")
        sy.dwait(e, "sp")
        sy.dwait(e, "wst")
    quant.a_s = hidb
    quant.s_s = szb.rearrange("p c t -> p (c t)")
    quant.n_s = ln1.rearrange("p u d -> p (u d)")
    quant.q2s = ln2s.rearrange("p u d -> p (u d)")

    cc_finish()
    xq_tok = hbfb.rearrange("p c t -> p (c t)")
    xqv = hbfb.rearrange("p c t -> p (c t)").rearrange("p (u d) -> p u d", u=TSUB)
    xq_fm = hbf2
    hq_tok = ln2s.rearrange("p u d -> p (u d)").bitcast(BF16)[:, 0:I]
    ln2f = ln2s.rearrange("p u d -> p (u d)")
    xcbf = xcb.rearrange("p c t -> p (c t)")
    szbf = szb.rearrange("p c t -> p (c t)")
    yacf = yac.rearrange("p c t -> p (c t)")
    dtbf = dtb.rearrange("p c t -> p (c t)")
    PGU = (P_a, P_b, P_c, P_d, P_e, P_f, P_g[:, 0:512], P_g[:, 512:1024])
    PD = (P_a, P_b, P_c, P_d, P_e, P_f, P_g[:, 0:512], P_g[:, 512:1024])

    def ln4_xquant(c):
        t0 = c * TC
        for ts in range(TSUB):
            par_ = ts % 2
            xsm = xsm1 if par_ == 0 else hidb[:, 0:D]
            chx = f"xsm{par_}"
            sy.wait("sp", "act", sy.xsm_rd[par_][0])
            sy.wait("sp", "dve", sy.xsm_rd[par_][1])
            sy.dwait("sp", f"outp{par_}", sy.xsm_out[par_])
            if par_ == 1:
                sy.wait("sp", "pool")
            n_x = sy.dma("sp", xsm, x2_h[t0 + ts * 128:t0 + (ts + 1) * 128, :], ch=chx)
            sy.dwait("act", chx, n_x)
            sy.dwait("dve", chx, n_x)
            if ts == 0:
                sy.wait("dve", "pool")
                sy.wait("dve", "act")
            ln_stats(xsm, D, MU(), RS())
            ln_apply(ln1[:, ts, :], xsm, MU(), RS())
            sy.xsm_rd[par_] = (sy.cnt["act"], sy.cnt["dve"])
        sy.wait("pool", "pe", sy.hq_pe_all)
        sy.wait("dve", "pe", sy.hq_pe_all)
        sy.wait("act", "pe", sy.hq_pe_all)
        sts = []
        for ts in range(TSUB):
            bufs = (hidb[:, ts * D:(ts + 1) * D], ln2f[:, ts * D:(ts + 1) * D],
                    xcbf[:, ts * D:(ts + 1) * D], szbf[:, ts * D:(ts + 1) * D])
            sts.append((quant_s1(ln1[:, ts, :], D // 16, sin_t[:, 0:1], 578 * ts, bufs), ts))
        sy.xq_pend = sts

    def xquant_s2():
        for st, tsp in sy.xq_pend:
            quant_s2(st, xq_tok[:, tsp * D:(tsp + 1) * D])
        sy.xq_pend = []

    def xq_transpose():
        for dch in range(DCH):
            sy.wait("pe", "dve")
            sy.wait("pe", "act")
            sy.wait("pe", "pool", quant.out_t)
            for ts in range(TSUB):
                sy.tick("pe", PE.transpose(P_fb[:, ts * 128:(ts + 1) * 128],
                                           xqv[:, ts, dch * 128:(dch + 1) * 128], idb[:]))
            sy.wait("act", "pe")
            sy.tick("act", Sc.activation(xq_fm[:, dch, :], P_fb[:, 0:TC], AF.Copy))

    sy.xq_pend = []
    if "B" in phases:
        sy.wait("act", "pool")
        for sl in range(8):
            for chn in (f"wsr{sl}", f"pwsr{sl}"):
                sy.dwait("pool", chn)
                sy.dwait("dve", chn)
        ln4_xquant(0)
        xquant_s2()
        xq_transpose()
    for c in (range(NCH) if "B" in phases else []):
        t0 = c * TC
        sy.wait("pe", "act")

        def emit_hq_tr(qbp, rotp, hqslp):
            sy.wait("pe", "pool", sy.hid_q[rotp])
            sy.wait("pe", "act", sy.hq_ev[0])
            hq2p = hqslp.rearrange("p (u i) -> p u i", u=TSUB)
            for ts in range(TSUB):
                for bi in range(2):
                    PB = P_eb if bi == 0 else P_fb
                    sy.tick("pe", PE.transpose(PB[:, 512 + ts * 128:512 + (ts + 1) * 128],
                                               hq2p[:, ts, bi * 128:(bi + 1) * 128], idb[:]))
            sy.hq_pe[rotp] = sy.cnt["pe"]
            sy.wait("act", "pe")
            for ts in range(TSUB):
                for bi in range(2):
                    PB = P_eb if bi == 0 else P_fb
                    sy.tick("act", Sc.activation(hqT[:, qbp * 2 + bi, ts * 128:(ts + 1) * 128],
                                                 PB[:, 512 + ts * 128:512 + (ts + 1) * 128], AF.Copy))
            sy.hq_ev = (sy.cnt["act"], sy.cnt["dve"])

        pend_tr = []
        pend_q = None
        for qb in range(I // 256):
            if len(pend_tr) >= 2:
                emit_hq_tr(*pend_tr.pop(0))
            par = qb % 2
            rot = qb % 4
            stg_g = wstg[:, :, par * 256:par * 256 + 256]
            stg_u = wstg2[:, :, par * 256:par * 256 + 256]
            chg, chu = f"wg{par}", f"wu{par}"
            qg, qu = ("sp", "act") if par == 0 else ("act", "sp")
            eg, eu = ("sp", "act") if par == 0 else ("act", "sp")
            sy.wait(eg, "pe", sy.gustrip_rd[par])
            sy.wait(eu, "pe", sy.gustrip_rd[par])
            if c == 0 and qb < 2:
                sy.dwait(eg, "wqst", sy.w2_st)
                sy.dwait(eu, "wqst", sy.w2_st)
                sy.wait(eg, "pe")
                sy.wait(eu, "pe")
            n_w = sy.dma(qg, stg_g, wgT_h[qb], ch=chg)
            n_w2 = sy.dma(qu, stg_u, wuT_h[qb], ch=chu)
            sy.dwait("pe", chg, n_w)
            sy.dwait("pe", chu, n_w2)
            sy.wait("pe", "act", sy.gups_ev[par][0])
            sy.wait("pe", "dve", sy.gups_ev[par][1])
            for ts in range(TSUB):
                for dch in range(DCH):
                    sy.tick("pe", PE.matmul(PGU[ts][:, 0:256], xq_fm[:, dch, ts * 128:(ts + 1) * 128],
                                            stg_g[:, dch, :], start=(dch == 0), stop=(dch == DCH - 1)))
                for dch in range(DCH):
                    sy.tick("pe", PE.matmul(PGU[4 + ts][:, 0:256], xq_fm[:, dch, ts * 128:(ts + 1) * 128],
                                            stg_u[:, dch, :], start=(dch == 0), stop=(dch == DCH - 1)))
            sy.gustrip_rd[par] = sy.cnt["pe"]
            hsl = hidb[:, rot * 1024:(rot + 1) * 1024]
            hqsl = hq_tok[:, rot * 1024:(rot + 1) * 1024]
            sy.wait("act", "pe")
            sy.wait("act", "pool", sy.hid_q[rot])
            if qb == 0:
                sy.wait("act", "dve")
            for ts in range(TSUB):
                sy.tick("act", Sc.activation(dump[:, ts * 256:ts * 256 + 256], PGU[ts][:, 0:256], AF.Silu))
            sy.wait("dve", "act")
            sy.wait("dve", "pe")
            sy.wait("dve", "pool", sy.hid_q[rot])
            for ts in range(TSUB):
                sy.tick("dve", V.tensor_tensor(hsl[ :, ts * 256:(ts + 1) * 256], dump[:, ts * 256:ts * 256 + 256],
                                               PGU[4 + ts][:, 0:256], ALU.mult))
            sy.gups_ev[par] = (sy.cnt["act"], sy.cnt["dve"])
            sy.wait("pool", "pe", sy.hq_pe[rot])
            bufs = (xcbf[:, rot * 1024:(rot + 1) * 1024], szbf[:, rot * 1024:(rot + 1) * 1024],
                    yacf[:, rot * 1024:(rot + 1) * 1024], dtbf[:, rot * 1024:(rot + 1) * 1024])
            st = quant_s1(hsl, TSUB * 16, sind_t[:, 0:1], 578 * rot, bufs)
            if pend_q is not None:
                stp, qbp, rotp, hqslp = pend_q
                sy.hid_q[rotp] = quant_s2(stp, hqslp)
                pend_tr.append((qbp, rotp, hqslp))
            pend_q = (st, qb, rot, hqsl)
        stp, qbp, rotp, hqslp = pend_q
        sy.hid_q[rotp] = quant_s2(stp, hqslp)
        pend_tr.append((qbp, rotp, hqslp))
        for t_ in pend_tr:
            emit_hq_tr(*t_)
        sy.hq_pe_all = sy.cnt["pe"]
        if c + 1 < NCH:
            ln4_xquant(c + 1)
        wstgf = wstg.rearrange("p c t -> p (c t)")
        wstg2f = wstg2.rearrange("p c t -> p (c t)")
        D_slots = [wstg3[:, 0:1024], wstg4[:, 0:1024]] + [
            wstgf[:, k * 1024:(k + 1) * 1024] for k in range(4)] + [
            wstg2f[:, k * 1024:(k + 1) * 1024] for k in range(2)]
        pe_dbase = sy.cnt["pe"]
        for sl in range(8):
            sy.dring_rd[sl] = max(sy.dring_rd[sl], pe_dbase)
        for ich in range(ICH):
            sl = ich % 8
            qd = ("sp", "act", "pool")[sl % 3]
            chw = (f"dsr{sl}" if qd != "pool" else f"pdsr{sl}")
            sy.wait(qd, "pe", sy.dring_rd[sl])
            stg = D_slots[sl]
            n_w = sy.dma(qd, stg,
                         wdT_h[ich * 128:(ich + 1) * 128, :], ch=chw)
            sy.dwait("pe", chw, n_w)
            if ich == 0:
                sy.wait("pe", "dve")
                sy.wait("pe", "act")
            for ts in range(TSUB):
                for hf in range(2):
                    sy.tick("pe", PE.matmul(PD[ts * 2 + hf][:, 0:512], hqT[:, ich, ts * 128:(ts + 1) * 128],
                                            stg[:, hf * 512:(hf + 1) * 512],
                                            start=(ich == 0), stop=(ich == ICH - 1)))
            sy.dring_rd[sl] = sy.cnt["pe"]
        if c + 1 < NCH:
            xquant_s2()
        sy.gustrip_rd = {0: sy.cnt["pe"], 1: sy.cnt["pe"]}
        for ts in range(TSUB):
            for hf in range(2):
                sy.wait("dve", "pe")
                sy.wait("dve", "act")
                sy.tick("dve", V.tensor_copy(dtbf.rearrange("p (u d) -> p u d", u=TSUB)[:, ts, hf * 512:(hf + 1) * 512], PD[ts * 2 + hf][:, 0:512]))
        if c + 1 < NCH:
            xq_transpose()
        dnout = dtbf.rearrange("p (u d) -> p u d", u=TSUB)
        for ts in range(TSUB):
            par_ = ts % 2
            xsm = xsm1 if par_ == 0 else hidb[:, 0:D]
            chx = f"xsm{par_}"
            sy.wait("sp", "act", sy.xsm_rd[par_][0])
            sy.wait("sp", "dve", sy.xsm_rd[par_][1])
            sy.dwait("sp", f"outp{par_}", sy.xsm_out[par_])
            if par_ == 1:
                sy.wait("sp", "pool")
            n_x = sy.dma("sp", xsm, x2_h[t0 + ts * 128:t0 + (ts + 1) * 128, :], ch=chx)
            ln_stats(dnout[:, ts, :], D, MU(), RS())
            ln_apply(dump[:, 0:D], dnout[:, ts, :], MU(), RS())
            sy.dwait("dve", chx, n_x)
            sy.wait("dve", "dve")
            sy.tick("dve", V.tensor_tensor(xsm, xsm, dump[:, 0:D], ALU.add))
            sy.wait("sp", "dve")
            sy.xsm_out[par_] = sy.dma("sp", out_h[t0 + ts * 128:t0 + (ts + 1) * 128, :], xsm, ch=f"outp{par_}")
            sy.xsm_rd[par_] = (sy.cnt["act"], sy.cnt["dve"])
    if "B" not in phases:
        sy.wait("sp", "dve")
        sy.wait("sp", "act")
        for c in range(NCH):
            sy.dma("sp", out_h[c * TC:(c + 1) * TC, :].rearrange("(u p) d -> p u d", p=128), ln1[:], ch="outp")
    sy.dwait("pool", "outp0")
    sy.dwait("pool", "outp1")
    sy.dwait("pool", "outp")
    sy.dwait("pool", "pool")
    return nc


_NC_CACHE = {}

def _build_cached():
    if "nc" not in _NC_CACHE:
        _NC_CACHE["nc"] = build(T=2048, TC=512)
    return _NC_CACHE["nc"]


def kernel(L0_attn_x, L0_attn_gamma, L0_attn_beta, L0_attn_W_in, L0_attn_conv_w,
           L0_attn_W_x, L0_attn_W_dt, L0_attn_b_dt, L0_attn_A_log, L0_attn_D_param,
           L0_attn_W_out, L0_ffn_gate_w, L0_ffn_up_w, L0_ffn_down_w, L0_ffn_s_in,
           L0_ffn_s_in_down):
    import ml_dtypes
    from concourse.bass_utils import run_bass_kernel_spmd
    nc = _build_cached()

    def fm(v):
        return np.ascontiguousarray(np.asarray(v, np.float32).reshape(8, 128).T)

    f32 = lambda a: np.ascontiguousarray(np.asarray(a, np.float32))
    shared = {
        "winT_h": f32(np.asarray(L0_attn_W_in).T),
        "wdtT_h": f32(np.asarray(L0_attn_W_dt).T),
        "wxT_h": f32(np.asarray(L0_attn_W_x).T),
        "woutT_h": f32(np.asarray(L0_attn_W_out).T),
        "convw_h": np.ascontiguousarray(
            np.asarray(L0_attn_conv_w, np.float32)[:, 0, :].reshape(8, 128, 3).transpose(1, 0, 2)),
        "gamma_h": fm(L0_attn_gamma),
        "beta_h": fm(L0_attn_beta),
        "bdt_h": fm(L0_attn_b_dt),
        "Dp_h": fm(L0_attn_D_param),
        "Alog_h": np.ascontiguousarray(
            np.asarray(L0_attn_A_log, np.float32).reshape(8, 128, 16).transpose(1, 0, 2)),

        "sin_h": np.full((128, 1), np.float32(np.asarray(L0_ffn_s_in).reshape(-1)[0]), np.float32),
        "sind_h": np.full((128, 1), np.float32(np.asarray(L0_ffn_s_in_down).reshape(-1)[0]), np.float32),
        "idb_h": np.eye(128, dtype=ml_dtypes.bfloat16),
        "idf_h": np.eye(128, dtype=np.float32),
    }
    x = np.asarray(L0_attn_x, np.float32)
    gw = np.asarray(L0_ffn_gate_w, np.float32)
    uw = np.asarray(L0_ffn_up_w, np.float32)
    dw = np.asarray(L0_ffn_down_w, np.float32)
    I8, D8 = 4096 // 8, 1024 // 8
    in_maps = [dict(shared, x_h=np.ascontiguousarray(x[b]),
                    gwsh_h=np.ascontiguousarray(gw[b * I8:(b + 1) * I8]),
                    uwsh_h=np.ascontiguousarray(uw[b * I8:(b + 1) * I8]),
                    dwsh_h=np.ascontiguousarray(dw[b * D8:(b + 1) * D8]))
               for b in range(8)]
    res = run_bass_kernel_spmd(nc, in_maps, core_ids=list(range(8)))
    return np.stack([np.asarray(r["out_h"], np.float32) for r in res.results], axis=0)


# version 29
# speedup vs baseline: 1.0862x; 1.0862x over previous
import numpy as np
import concourse.bass as bass
import concourse.mybir as mybir
from concourse.bass_isa import ReduceOp
from concourse import library_config as _LC

F32 = mybir.dt.float32
BF16 = mybir.dt.bfloat16
AF = mybir.ActivationFunctionType
ALU = mybir.AluOpType
AX = mybir.AxisListType

D = 1024
DCH = D // 128
E2 = 2048
ECH = E2 // 128
S = 16
I = 4096
ICH = I // 128
EPS = 1e-5
C2688 = 2688.0

_REG = {}


def register_custom_ops():
    if _REG:
        return _REG["qlow"], _REG["qhigh"]
    import concourse.dve_ops as dve_ops
    from concourse.dve_op import DveOp
    from concourse.dve_spec import Spec, Src0, Src1, C0, C1, C2, C3, lower

    def _f(x):
        return x.astype(np.float32) if hasattr(x, "astype") else np.float32(x)

    qlow_body = (Src0 >= C0) + (Src0 >= C1) + (Src0 >= C2) + (Src0 >= C3)

    def qlow_ref(in0, in1, s0, s1, imm2):
        t = _f(in0)
        return ((t >= _f(s0)).astype(np.float32) + (t >= _f(s1)).astype(np.float32)
                + (t >= np.float32(imm2)).astype(np.float32)
                + (t >= _f(in1)).astype(np.float32))

    c4 = (Src0 >= C0); c5 = (Src0 >= C1); c6 = (Src0 >= C2)
    u_ = (c4 + c5) + (c6 + c6)
    qhigh_body = Src1 + (u_ + u_)

    def qhigh_ref(in0, in1, s0, s1, imm2):
        t = _f(in0)
        u = ((t >= _f(s0)).astype(np.float32) + (t >= _f(s1)).astype(np.float32)
             + 2.0 * (t >= np.float32(imm2)).astype(np.float32))
        return _f(in1) + 2.0 * u

    out = []
    for name, body, ref in (("ANT_QLOW", qlow_body, qlow_ref),
                            ("ANT_QHIGH", qhigh_body, qhigh_ref)):
        op = next((o for o in dve_ops.OPS if o.name == name), None)
        if op is None:
            spec = Spec(body=body)
            sha = {}
            for ver in ("v3", "v4"):
                try:
                    sha[ver] = lower(spec, ver=ver, perf_en=False, name=name).sha(ver)
                except Exception:
                    pass
            op = DveOp(name, spec, subdim=False, uops_sha=sha, reference=ref)
            dve_ops.OPS.append(op)
            dve_ops._SUB_OPCODE_FOR_NAME[name] = max(dve_ops._SUB_OPCODE_FOR_NAME.values()) + 1
            assert max(dve_ops._SUB_OPCODE_FOR_NAME.values()) < 0x20
            dve_ops.CUSTOM_DVE_REFS[name] = op.ref
            dve_ops.CUSTOM_DVE_SPECS[name] = spec
        out.append(op)
    _REG["qlow"], _REG["qhigh"] = out
    return out


class Sync:
    def __init__(self, nc):
        self.nc = nc
        self.eng = {"pe": nc.tensor, "act": nc.scalar, "dve": nc.vector,
                    "pool": nc.gpsimd, "sp": nc.sync}
        self.sem = {k: nc.alloc_semaphore(f"prog_{k}") for k in self.eng}
        self.cnt = {k: 0 for k in self.eng}
        self.seen = {(a, b): 0 for a in self.eng for b in self.eng}
        self.dq = {"sp": nc.sync, "pool": nc.gpsimd, "act": nc.scalar,
                   "vec": nc.vector, "ten": nc.tensor}
        self.dsem = {}
        self.dcnt = {}
        self.dseen = {}
        self.cc_sem = nc.alloc_semaphore("cc_sem")
        self.cc_cnt = 0

    def tick(self, key, instr):
        self.cnt[key] += 1
        instr.then_inc(self.sem[key], 1)
        return self.cnt[key]

    def wait(self, waiter, producer, upto=None):
        v = self.cnt[producer] if upto is None else upto
        if self.seen[(waiter, producer)] >= v or v <= 0:
            return
        self.eng[waiter].wait_ge(self.sem[producer], v)
        self.seen[(waiter, producer)] = v

    SERIAL_DMA = False

    def _ch(self, ch):
        if self.SERIAL_DMA:
            ch = {"wqst": "pool"}.get(ch, "sp") if ch not in ("sp", "pool", "act") else ch
        if ch not in self.dsem:
            self.dsem[ch] = self.nc.alloc_semaphore(f"dma_{ch}")
            self.dcnt[ch] = 0
            for a in self.eng:
                self.dseen[(a, ch)] = 0
        return ch

    def dma(self, qkey, dst, src, ch=None):
        if self.SERIAL_DMA:
            ch = self._ch(qkey)
            self.dwait(qkey, ch)
        else:
            ch = self._ch(ch or qkey)
        self.dq[qkey].dma_start(dst, src).then_inc(self.dsem[ch], 16)
        self.dcnt[ch] += 16
        return self.dcnt[ch]

    def dmat(self, dst, src, ch):
        ch = self._ch(ch)
        self.nc.sync.dma_start_transpose(dst, src).then_inc(self.dsem[ch], 16)
        self.dcnt[ch] += 16
        return self.dcnt[ch]

    def dwait(self, waiter, ch, upto=None):
        if self.SERIAL_DMA and upto is not None:
            upto = None
        ch = self._ch(ch)
        v = self.dcnt[ch] if upto is None else upto
        if self.dseen[(waiter, ch)] >= v or v <= 0:
            return
        self.eng[waiter].wait_ge(self.dsem[ch], v)
        self.dseen[(waiter, ch)] = v


def build(T=2048, TC=512, phases=("W1", "A", "W2", "B"), n_cores=8):
    QLOW, QHIGH = register_custom_ops()
    NCH = T // TC
    TSUB = TC // 128
    nc = bass.Bass("TRN2", target_bir_lowering=False)
    sy = Sync(nc)
    sy.xsm_rd = {0: (0, 0), 1: (0, 0)}
    sy.wstrip_rd = {0: 0, 1: 0}
    sy.gustrip_rd = {0: 0, 1: 0}
    sy.gups_ev = {0: (0, 0), 1: (0, 0)}
    sy.tr_ev = {0: 0, 1: 0}
    sy.w2_st = 0
    sy.hid_q = {0: 0, 1: 0, 2: 0, 3: 0}
    sy.hq_pe = {0: 0, 1: 0, 2: 0, 3: 0}
    sy.aring_rd = [0] * 8
    sy.dring_rd = [0] * 8
    sy.win_ev = {0: 0, 1: 0}
    sy.hq_pe_all = 0
    sy.hq_ev = (0, 0)
    sy.xsm_out = {0: 0, 1: 0}
    V, Sc, PE, GP = nc.vector, nc.scalar, nc.tensor, nc.gpsimd

    dram = lambda n, shp, dt=F32, kind="ExternalInput": nc.dram_tensor(n, shp, dt, kind=kind).ap()
    x_h = dram("x_h", [T, D])
    winT_h = dram("winT_h", [D, E2])
    wdtT_h = dram("wdtT_h", [D, D])
    wxT_h = dram("wxT_h", [D, 2 * S])
    woutT_h = dram("woutT_h", [D, D])
    convw_h = dram("convw_h", [128, DCH, 3])
    gamma_h = dram("gamma_h", [128, DCH])
    beta_h = dram("beta_h", [128, DCH])
    bdt_h = dram("bdt_h", [128, DCH])
    Dp_h = dram("Dp_h", [128, DCH])
    Alog_h = dram("Alog_h", [128, DCH, S])
    gwsh_h = dram("gwsh_h", [I // 8, D])
    uwsh_h = dram("uwsh_h", [I // 8, D])
    dwsh_h = dram("dwsh_h", [D // 8, I])
    sin_h = dram("sin_h", [128, 1])
    sind_h = dram("sind_h", [128, 1])
    idb_h = dram("idb_h", [128, 128], BF16)
    idf_h = dram("idf_h", [128, 128])
    out_h = dram("out_h", [T, D], kind="ExternalOutput")
    x2_h = dram("x2_h", [T, D], kind="Internal")
    winb_h = dram("winb_h", [ECH, 128, DCH, 128], BF16, kind="Internal")
    wdtb_h = dram("wdtb_h", [DCH, 128, DCH, 128], BF16, kind="Internal")
    woutb_h = dram("woutb_h", [DCH, 2, 128, 512], BF16, kind="Internal")
    wgT_h = dram("wgT_h", [I // 256, 128, DCH, 256], BF16, kind="Internal")
    wuT_h = dram("wuT_h", [I // 256, 128, DCH, 256], BF16, kind="Internal")
    wdT_h = dram("wdT_h", [I, D], BF16, kind="Internal")
    gsin_h = dram("gsin_h", [1, 4], kind="Internal")
    gsout_h = dram("gsout_h", [1, 4], kind="Internal")
    gsh_b = dram("gsh_b", [D, I // 8], BF16, kind="Internal")
    ush_b = dram("ush_b", [D, I // 8], BF16, kind="Internal")
    dsh_b = dram("dsh_b", [I, D // 8], BF16, kind="Internal")
    gg_h = dram("gg_h", [8, D, I // 8], BF16, kind="Internal")
    ug_h = dram("ug_h", [8, D, I // 8], BF16, kind="Internal")
    dg_h = dram("dg_h", [8, I, D // 8], BF16, kind="Internal")

    sb = lambda n, shp, dt=F32: nc.alloc_sbuf_tensor(n, shp, dt).ap()
    convw = sb("convw", [128, DCH, 3])
    gamma = sb("gamma", [128, DCH]); beta = sb("beta", [128, DCH])
    bdt = sb("bdt", [128, DCH]); Dp = sb("Dp", [128, DCH])
    Acol = sb("Acol", [128, DCH, S])
    sin_t = sb("sin_t", [128, 1]); sind_t = sb("sind_t", [128, 1])
    idb = sb("idb", [128, 128], BF16)
    idf = sb("idf", [128, 128])
    selB = sb("selB", [2 * S, 128])
    selC = sb("selC", [2 * S, 128])
    ones32 = sb("ones32", [2 * S, 128])
    be_t = sb("be_t", [128, ECH])
    st_t = sb("st_t", [128, DCH, S])
    halo = sb("halo", [128, DCH, 1])
    stat = sb("stat", [128, 16])
    gs_t = sb("gs_t", [128, 4])
    cEPS = sb("cEPS", [128, 1])
    cONE = sb("cONE", [128, 1])
    dump = sb("dump", [128, 1024])
    arenaA = sb("arenaA", [128, 8192])
    wldf = arenaA[:, 0:2048]
    xxb = arenaA[:, 2048:2048 + DCH * (TC + 2)].rearrange("p (c t) -> p c t", c=DCH)
    _o = 2048 + DCH * (TC + 2)
    dA2 = [arenaA[:, _o:_o + TC], arenaA[:, _o + TC:_o + 2 * TC]]; _o += 2 * TC
    dBb = arenaA[:, _o:_o + TC]; _o += TC
    hsc = sb("hsc", [128, TC])
    p2 = [sb("p2a", [128, TC]), sb("p2b", [128, TC])]
    assert _o <= 8192, _o
    hqT = arenaA.bitcast(BF16).rearrange("p (c t) -> p c t", c=ICH)
    xsm1 = sb("xsm1", [128, D])
    ln1 = sb("ln1", [128, TSUB, D])
    szb = sb("szb", [128, DCH, TC])
    xcb = sb("xcb", [128, DCH, TC])
    dtb = sb("dtb", [128, DCH, TC])
    yac = sb("yac", [128, DCH, TC])
    ln2s = sb("ln2s", [128, TSUB, D])
    hbfb = sb("hbfb", [128, DCH, TC], BF16)
    hbf2 = sb("hbf2", [128, DCH, TC], BF16)
    hfmh = sb("hfmh", [128, DCH, 1], BF16)
    hidb = sb("hidb", [128, I])
    wstg = sb("wstg", [128, DCH, 512], BF16)
    wstg2 = sb("wstg2", [128, DCH, 512], BF16)
    wstg3 = sb("wstg3", [128, 1024], BF16)
    wstg4 = sb("wstg4", [128, 1024], BF16)
    qsml = sb("qsml", [128, 2320])
    bct = qsml[0:2 * S, 0:TC]
    xh1 = qsml[0:1, 1024:2048]
    lh1 = qsml[0:1, 0:1024]
    lh2 = dump[0:1, 0:1024]
    lh2b = yac.rearrange("p c t -> p (c t)").bitcast(BF16)[0:1, 0:1024]
    wxT = sb("wxT", [128, DCH, 2 * S], BF16)

    psa = lambda n, shp: nc.alloc_psum_tensor(n, shp, F32).ap()
    P_a = psa("P_a", [128, 512])
    P_b = psa("P_b", [128, 512])
    P_c = psa("P_c", [128, 512])
    P_d = psa("P_d", [128, 512])
    P_e = psa("P_e", [128, 512])
    P_f = psa("P_f", [128, 512])
    P_g = psa("P_g", [128, 1024])
    P_fb = P_f.bitcast(BF16)
    P_eb = P_e.bitcast(BF16)

    def ln_stats(src_ap, n_feat, mu_col, rs_col):
        sy.wait("act", "dve")
        sy.tick("act", Sc.activation(dump[0:src_ap.shape[0], 0:n_feat], src_ap, AF.Square,
                                     accum_out=stat[0:src_ap.shape[0], 2:3]))
        sy.wait("dve", "dve")
        sy.tick("dve", V.reduce_sum(stat[0:src_ap.shape[0], 0:1], src_ap, AX.X))
        sy.wait("dve", "act")
        sy.wait("dve", "dve")
        sy.tick("dve", V.tensor_scalar(mu_col, stat[0:src_ap.shape[0], 0:1], 1.0 / n_feat, None, ALU.mult))
        sy.tick("dve", V.tensor_scalar(stat[0:src_ap.shape[0], 3:4], stat[0:src_ap.shape[0], 2:3], 1.0 / n_feat, None, ALU.mult))
        sy.wait("dve", "dve")
        sy.tick("dve", V.tensor_tensor(stat[0:src_ap.shape[0], 4:5], mu_col, mu_col, ALU.mult))
        sy.wait("dve", "dve")
        sy.tick("dve", V.tensor_tensor(stat[0:src_ap.shape[0], 5:6], stat[0:src_ap.shape[0], 3:4], stat[0:src_ap.shape[0], 4:5], ALU.subtract))
        sy.wait("act", "dve")
        sy.tick("act", Sc.activation(stat[0:src_ap.shape[0], 6:7], stat[0:src_ap.shape[0], 5:6], AF.Sqrt, bias=cEPS[0:src_ap.shape[0], :]))
        sy.wait("dve", "act")
        sy.tick("dve", V.reciprocal(rs_col, stat[0:src_ap.shape[0], 6:7]))

    def ln_apply(dst_ap, src_ap, mu_col, rs_col):
        sy.wait("dve", "dve")
        sy.tick("dve", V.tensor_scalar(dst_ap, src_ap, mu_col, rs_col, ALU.subtract, ALU.mult))

    MU = lambda p=128: stat[0:p, 7:8]
    RS = lambda p=128: stat[0:p, 8:9]

    def quant_s1(src_ap, nblk, gs_col, sq_off, bufs):
        p = src_ap.shape[0]
        o = sq_off
        amax = qsml[0:p, o:o + nblk]; o += nblk
        sc = qsml[0:p, o:o + nblk]; o += nblk
        b1 = qsml[0:p, o:o + nblk]; o += nblk
        g = qsml[0:p, o:o + nblk]; o += nblk
        r = qsml[0:p, o:o + nblk]; o += nblk
        n1 = qsml[0:p, o:o + nblk]; o += nblk
        n2 = qsml[0:p, o:o + nblk]; o += nblk
        dq = qsml[0:p, o:o + nblk]; o += nblk
        rdq = qsml[0:p, o:o + nblk]; o += nblk
        src3 = src_ap.rearrange("p (j k) -> p j k", k=16)
        a_s = bufs[0][0:p, 0:nblk * 16]
        s_s = bufs[1][0:p, 0:nblk * 16]
        n_s = bufs[2][0:p, 0:nblk * 16]
        q2s = bufs[3][0:p, 0:nblk * 16]
        sy.wait("act", "dve")
        sy.wait("act", "pool", quant.out_t)
        sgn_t = sy.tick("act", Sc.sign(n_s, src_ap))
        sy.wait("dve", "dve")
        sy.tick("dve", V.tensor_reduce(amax, src3, AX.X, ALU.max, apply_absolute_value=True))
        sy.wait("dve", "dve")
        sy.tick("dve", V.tensor_scalar(stat[0:p, 9:10], gs_col, 1.0 / 6.0, None, ALU.mult))
        sy.wait("dve", "dve")
        sy.tick("dve", V.tensor_scalar(sc, amax, stat[0:p, 9:10], 448.0, ALU.mult, ALU.min))
        sy.wait("dve", "dve")
        sy.tick("dve", V.tensor_scalar(b1, sc, 8.0, 1.0, ALU.is_ge, ALU.add))
        for thr in (16.0, 32.0, 64.0, 128.0, 256.0):
            sy.wait("dve", "dve")
            sy.tick("dve", V.tensor_scalar(g, sc, thr, 1.0, ALU.is_ge, ALU.add))
            sy.wait("dve", "dve")
            sy.tick("dve", V.tensor_tensor(b1, b1, g, ALU.mult))
        sy.wait("dve", "dve")
        sy.tick("dve", V.tensor_scalar(g, b1, 0.5, None, ALU.mult))
        sy.wait("dve", "dve")
        sy.tick("dve", V.reciprocal(n2, g))
        sy.wait("dve", "dve")
        sy.tick("dve", V.tensor_tensor(r, sc, n2, ALU.mult))
        sy.wait("dve", "dve")
        sy.tick("dve", V.tensor_scalar(n1, r, 8.5, None, ALU.is_ge))
        for j in range(1, 8):
            sy.wait("dve", "dve")
            sy.tick("dve", V.scalar_tensor_tensor(n1, r, 8.5 + j, n1, ALU.is_ge, ALU.add))
        sy.wait("dve", "dve")
        sy.tick("dve", V.scalar_tensor_tensor(sc, n1, 8.0, g, ALU.add, ALU.mult))
        sy.wait("dve", "dve")
        sy.tick("dve", V.reciprocal(stat[0:p, 15:16], gs_col))
        sy.wait("dve", "dve")
        sy.tick("dve", V.tensor_scalar(dq, sc, stat[0:p, 15:16], None, ALU.mult))
        sy.wait("dve", "dve")
        sy.tick("dve", V.reciprocal(rdq, dq))
        sy.tick("dve", V.tensor_scalar(dq, dq, 0.5, None, ALU.mult))
        sy.wait("dve", "dve")
        rdq_b = rdq.rearrange("p (j o) -> p j o", o=1).broadcast_to([p, nblk, 16])
        dq_b = dq.rearrange("p (j o) -> p j o", o=1).broadcast_to([p, nblk, 16])
        sy.wait("pool", "dve")
        sy.wait("pool", "act")
        sy.wait("pool", "pool")
        sy.tick("pool", GP.tensor_tensor(a_s.rearrange("p (j k) -> p j k", k=16), src3, rdq_b, ALU.mult))
        sy.wait("pool", "pool")
        gsq = sy.tick("pool", GP.tensor_tensor(s_s, a_s, a_s, ALU.mult))
        sy.wait("pool", "act", sgn_t)
        sy.wait("pool", "pool")
        gdqs = sy.tick("pool", GP.tensor_tensor(a_s.rearrange("p (j k) -> p j k", k=16),
                                                n_s.rearrange("p (j k) -> p j k", k=16), dq_b, ALU.mult))
        return dict(a_s=a_s, s_s=s_s, n_s=n_s, q2s=q2s, gdqs=gdqs)

    def quant_s2(st, dst_bf_ap):
        s_s, n_s, q2s, a_s = st["s_s"], st["n_s"], st["q2s"], st["a_s"]
        sy.wait("dve", "pool", st["gdqs"])
        sy.tick("dve", V.tensor_scalar(n_s, s_s, 0.0625, None, ALU.is_ge))
        for thr in (0.5625, 1.5625, 3.0625):
            sy.wait("dve", "dve")
            sy.tick("dve", V.scalar_tensor_tensor(n_s, s_s, thr, n_s, ALU.is_ge, ALU.add))
        sy.wait("dve", "dve")
        sy.tick("dve", V.tensor_scalar(q2s, s_s, 25.0, 2.0, ALU.is_ge, ALU.mult))
        sy.wait("dve", "dve")
        sy.tick("dve", V.scalar_tensor_tensor(q2s, s_s, 6.25, q2s, ALU.is_ge, ALU.add))
        sy.wait("dve", "dve")
        sy.tick("dve", V.scalar_tensor_tensor(q2s, s_s, 12.25, q2s, ALU.is_ge, ALU.add))
        sy.wait("dve", "dve")
        sy.tick("dve", V.scalar_tensor_tensor(q2s, q2s, 2.0, n_s, ALU.mult, ALU.add))
        sy.wait("pool", "dve")
        sy.wait("pool", "pool")
        quant.out_t = sy.tick("pool", GP.tensor_tensor(dst_bf_ap, q2s, a_s, ALU.mult))
        return quant.out_t

    def quant(dst_bf_ap, src_ap, nblk, gs_col, sq_off):
        quant_s2(quant_s1(src_ap, nblk, gs_col, sq_off,
                          (quant.a_s, quant.s_s, quant.n_s, quant.q2s)), dst_bf_ap)

    quant.out_t = 0
    for t, h in ((convw, convw_h), (gamma, gamma_h), (beta, beta_h), (bdt, bdt_h),
                 (Dp, Dp_h), (Acol, Alog_h), (sin_t, sin_h), (sind_t, sind_h),
                 (idb, idb_h), (idf, idf_h)):
        sy.dma("sp", t[:], h[:])
    sy.dwait("act", "sp")
    sy.tick("act", Sc.activation(dump[:, 0:DCH * S], Acol.rearrange("p c s -> p (c s)"), AF.Exp))
    sy.wait("dve", "act")
    sy.tick("dve", V.tensor_scalar(Acol.rearrange("p c s -> p (c s)"), dump[:, 0:DCH * S], -1.0, None, ALU.mult))
    sy.tick("dve", V.memset(st_t[:], 0.0))
    sy.tick("dve", V.memset(halo[:], 0.0))
    sy.tick("dve", V.memset(ones32[:], 1.0))
    sy.tick("dve", V.memset(cEPS[:], EPS))
    sy.tick("dve", V.memset(cONE[:], 1.0))

    do_W1 = "W1" in phases
    wbf = wstg.rearrange("p c t -> p (c t)")
    for dch in (range(DCH) if do_W1 else []):
        sy.wait("sp", "dve")
        sy.wait("sp", "pe")
        sy.dwait("sp", "sp")
        n = sy.dma("sp", wldf[:, 0:E2], winT_h[dch * 128:(dch + 1) * 128, :])
        sy.dwait("dve", "sp", n)
        sy.dwait("dve", "wst")
        sy.tick("dve", V.tensor_scalar(wbf[:, 0:E2], wldf[:, 0:E2], gamma[:, dch:dch + 1], None, ALU.mult))
        sy.dwait("pe", "sp", n)
        for eblk in range(4):
            P_mm = (P_a, P_b, P_c, P_d)[eblk]
            sy.tick("pe", PE.matmul(P_mm[0:1, 0:512], beta[:, dch:dch + 1],
                                    wldf[:, eblk * 512:(eblk + 1) * 512],
                                    start=(dch == 0), stop=(dch == DCH - 1)))
        sy.wait("act", "dve")
        sy.dma("act", winb_h[:, :, dch, :].rearrange("e p k -> p e k"),
               wbf[:, 0:E2].bitcast(BF16).rearrange("p (e k) -> p e k", k=128), ch="wst")
    sy.wait("dve", "pe")
    for eblk in (range(4) if do_W1 else []):
        P_mm = (P_a, P_b, P_c, P_d)[eblk]
        sy.tick("dve", V.tensor_copy(dump[0:1, 0:512], P_mm[0:1, 0:512]))
        sy.wait("pe", "dve")
        for j in range(4):
            ech = eblk * 4 + j
            sy.tick("pe", PE.transpose(P_e[:, ech:ech + 1], dump[0:1, j * 128:(j + 1) * 128],
                                       idf[0:1, 0:1]))
        sy.wait("dve", "pe")
    sy.tick("dve", V.tensor_copy(be_t[:, 0:ECH], P_e[:, 0:ECH]))
    for wi, (src_h, dst_h) in enumerate((((wdtT_h, wdtb_h), (woutT_h, woutb_h)) if do_W1 else ())):
        for dch in range(DCH):
            sy.wait("sp", "dve")
            sy.dwait("sp", "sp")
            n = sy.dma("sp", wldf[:, 0:D], src_h[dch * 128:(dch + 1) * 128, :])
            sy.dwait("dve", "sp", n)
            sy.dwait("dve", "wst")
            sy.tick("dve", V.tensor_copy(wbf[:, 0:D], wldf[:, 0:D]))
            sy.wait("act", "dve")
            if wi == 0:
                sy.dma("act", wdtb_h[:, :, dch, :].rearrange("e p k -> p e k"),
                       wbf[:, 0:D].bitcast(BF16).rearrange("p (e k) -> p e k", k=128), ch="wst")
            else:
                sy.dma("act", woutb_h[dch].rearrange("h p k -> p h k"),
                       wbf[:, 0:D].bitcast(BF16).rearrange("p (h k) -> p h k", k=512), ch="wst")
    for dch in (range(DCH) if do_W1 else []):
        sy.wait("sp", "dve")
        sy.dwait("sp", "sp")
        n = sy.dma("sp", wldf[:, 0:2 * S], wxT_h[dch * 128:(dch + 1) * 128, :])
        sy.dwait("dve", "sp", n)
        sy.tick("dve", V.tensor_copy(wxT[:, dch, :], wldf[:, 0:2 * S]))

    def wq_pass1(w_h, rows, cols, midx):
        nhalf0 = 1 if cols <= 2048 else 2
        csz0 = cols // nhalf0
        nsl = rows // 128
        wbufs0 = [wldf, xcb.rearrange("p c t -> p (c t)")[:, 0:2048]]
        k = 0
        for sl in range(nsl):
            for hf in range(nhalf0):
                par = k % 2
                chw = f"wq{par}"
                sy.wait("sp", "dve", sy.w2rd[par])
                sy.dwait("sp", chw)
                n_w = sy.dma("sp", wbufs0[par][:, 0:csz0],
                             w_h[sl * 128:(sl + 1) * 128, hf * csz0:(hf + 1) * csz0], ch=chw)
                sy.dwait("dve", chw, n_w)
                sy.wait("dve", "dve")
                sy.tick("dve", V.tensor_reduce(qsml[:, k:k + 1], wbufs0[par][:, 0:csz0], AX.X, ALU.max,
                                               apply_absolute_value=True))
                sy.w2rd[par] = sy.cnt["dve"]
                k += 1
        sy.wait("dve", "dve")
        sy.tick("dve", V.tensor_reduce(stat[:, 12 + midx:13 + midx], qsml[:, 0:k], AX.X, ALU.max))

    def wq_pass2(w_h, wq_h, rows, cols, gs_col):
        import os as _os
        nhalf = 1 if cols <= 2048 else 2
        csz = cols // nhalf
        nsl = rows // 128
        wbufs = [wldf, xcb.rearrange("p c t -> p (c t)")[:, 0:2048]]
        k = 0
        for sl in range(nsl):
            for hf in range(nhalf):
                par = k % 2
                wsrc = wbufs[par]
                chw = f"wq{par}"
                sy.wait("sp", "dve", sy.w2rd[par])
                sy.wait("sp", "act", sy.w2rda[par])
                sy.dwait("sp", chw)
                n_w = sy.dma("sp", wsrc[:, 0:csz], w_h[sl * 128:(sl + 1) * 128, hf * csz:(hf + 1) * csz], ch=chw)
                sy.dwait("dve", chw, n_w)
                sy.dwait("act", chw, n_w)
                wqb = wstg.rearrange("p c t -> p (c t)")[:, 0:csz]
                sy.wait("dve", "pe")
                quant(wqb, wsrc[:, 0:csz], csz // 16, gs_col, 0)
                sy.w2rd[par] = sy.cnt["dve"]
                sy.w2rda[par] = sy.cnt["act"]
                q_t = sy.cnt["dve"]
                nb = csz // 128
                sy.wait("pe", "dve", q_t)
                sy.wait("pe", "pool", quant.out_t)
                sy.dwait("act", "wqst", sy.w2_st)
                for bi in range(nb):
                    PB = P_fb if bi % 2 == 0 else P_eb
                    sy.wait("pe", "act", sy.tr_ev[bi % 2])
                    sy.tick("pe", PE.transpose(PB[:, 0:128],
                                               wqb[:, bi * 128:(bi + 1) * 128], idb[:]))
                    sy.wait("act", "pe")
                    sy.tick("act", Sc.activation(wstg2.rearrange("p c t -> p (c t)")[:, bi * 128:(bi + 1) * 128],
                                                 PB[:, 0:128], AF.Copy))
                    sy.tr_ev[bi % 2] = sy.cnt["act"]
                sy.wait("pool", "act")
                sy.w2_st = sy.dma("pool", wq_h[hf * csz:(hf + 1) * csz, sl * 128:(sl + 1) * 128]
                                  .rearrange("(b q) r -> q b r", q=128),
                                  wstg2.rearrange("p c t -> p (c t)")[:, 0:csz].rearrange("q (b r) -> q b r", r=128),
                                  ch="wqst")
                k += 1

    if "W2" in phases or "W2a" in phases:
        for e in ("dve", "act", "pe", "pool"):
            sy.dwait(e, "wst")
            sy.dwait(e, "sp")
        sy.w2rd = {0: sy.cnt["dve"], 1: sy.cnt["dve"]}
        sy.w2rda = {0: sy.cnt["act"], 1: sy.cnt["act"]}
        wq_pass1(gwsh_h, I // 8, D, 0)
        wq_pass1(uwsh_h, I // 8, D, 1)
        wq_pass1(dwsh_h, D // 8, I, 2)
        sy.wait("pe", "dve")
        sy.tick("pe", PE.transpose(P_g[0:1, 0:128], stat[:, 12:13], idf[:]))
        sy.tick("pe", PE.transpose(P_g[0:1, 128:256], stat[:, 13:14], idf[:]))
        sy.tick("pe", PE.transpose(P_g[0:1, 256:384], stat[:, 14:15], idf[:]))
        sy.wait("dve", "pe")
        for m in range(3):
            sy.tick("dve", V.tensor_reduce(dump[0:1, 200 + m:201 + m], P_g[0:1, m * 128:(m + 1) * 128], AX.X, ALU.max))
        sy.wait("dve", "dve")
        sy.tick("dve", V.memset(dump[0:1, 203:204], 1.0))
        sy.wait("pool", "dve")
        sy.dma("pool", gsin_h[:], dump[0:1, 200:204], ch="ccio")
        sy.dwait("pool", "ccio")
        if "B" in phases:
            GP.collective_compute("AllReduce", ALU.max, replica_groups=[list(range(n_cores))],
                                  ins=[gsin_h[:].opt()], outs=[gsout_h[:].opt()]).then_inc(sy.cc_sem, 1)
            sy.cc_cnt += 1
            sy.eng["pool"].wait_ge(sy.cc_sem, sy.cc_cnt)
        else:
            sy.dma("pool", gsout_h[:], gsin_h[:], ch="ccio")
        n_gs = sy.dma("pool", dump[0:1, 208:212], gsout_h[:], ch="ccio")
        sy.dwait("pe", "ccio", n_gs)
        sy.tick("pe", PE.matmul(P_g[:, 0:4], ones32[0:1, :], dump[0:1, 208:212], start=True, stop=True))
        sy.wait("dve", "pe")
        sy.tick("dve", V.reciprocal(stat[:, 8:9], P_g[:, 0:1]))
        sy.tick("dve", V.reciprocal(stat[:, 9:10], P_g[:, 1:2]))
        sy.tick("dve", V.reciprocal(stat[:, 10:11], P_g[:, 2:3]))
        sy.wait("dve", "dve")
        sy.tick("dve", V.tensor_scalar(gs_t[:, 0:1], stat[:, 8:9], C2688, None, ALU.mult))
        sy.tick("dve", V.tensor_scalar(gs_t[:, 1:2], stat[:, 9:10], C2688, None, ALU.mult))
        sy.tick("dve", V.tensor_scalar(gs_t[:, 2:3], stat[:, 10:11], C2688, None, ALU.mult))
        sy.wait("dve", "dve")
    if "W2" in phases:
        quant.a_s = hidb
        quant.s_s = szb.rearrange("p c t -> p (c t)")
        quant.n_s = ln1.rearrange("p u d -> p (u d)")
        quant.q2s = ln2s.rearrange("p u d -> p (u d)")
        wq_pass2(gwsh_h, gsh_b, I // 8, D, gs_t[:, 0:1])
        wq_pass2(uwsh_h, ush_b, I // 8, D, gs_t[:, 1:2])
        wq_pass2(dwsh_h, dsh_b, D // 8, I, gs_t[:, 2:3])
        sy.dwait("pool", "wqst")
        if "B" in phases:
            for src, dst in ((gsh_b, gg_h), (ush_b, ug_h), (dsh_b, dg_h)):
                GP.collective_compute("AllGather", ALU.bypass, replica_groups=[list(range(n_cores))],
                                      ins=[src[:].opt()], outs=[dst[:].opt()]).then_inc(sy.cc_sem, 1)
                sy.cc_cnt += 1

    sy.cc_issued = False

    def cc_issue():
        if sy.cc_issued or not ("W2" in phases and "B" in phases):
            return
        sy.cc_issued = True
        sy.eng["sp"].wait_ge(sy.cc_sem, sy.cc_cnt)
        sy.eng["act"].wait_ge(sy.cc_sem, sy.cc_cnt)
        for sh in range(8):
            for half in range(2):
                sy.dma("sp", wgT_h[2 * sh + half],
                       gg_h[sh][:, half * 256:(half + 1) * 256]
                       .rearrange("(c p) e -> p c e", p=128), ch="rp")
                sy.dma("act", wuT_h[2 * sh + half],
                       ug_h[sh][:, half * 256:(half + 1) * 256]
                       .rearrange("(c p) e -> p c e", p=128), ch="rp")
            sy.dma("sp", wdT_h[:, sh * (D // 8):(sh + 1) * (D // 8)], dg_h[sh], ch="rp")

    def cc_finish():
        cc_issue()
        if "W2" in phases and "B" in phases:
            sy.dwait("sp", "rp")
            sy.dwait("act", "rp")
            sy.dwait("pool", "rp")


    sy.dwait("sp", "wst")
    if "W2" in phases:
        for e in ("sp", "act", "dve", "pe"):
            sy.wait(e, "pool")
            sy.dwait(e, "wq0")
            sy.dwait(e, "wq1")
            sy.dwait(e, "wqst")
    if "A" in phases:
        sy.wait("sp", "pool")
        sy.wait("act", "pool")
        for dch in range(DCH):
            sy.n_wout0 = sy.dma("sp", wstg[:, dch, :], woutb_h[dch, 0], ch="wout0")
            sy.n_wout1 = sy.dma("act", wstg2[:, dch, :], woutb_h[dch, 1], ch="wout1")
    lastxx = sy.cnt["dve"]
    for c in (range(NCH) if "A" in phases else []):
        t0 = c * TC
        if c == 2:
            cc_issue()
        sy.dwait("sp", "pool")
        sy.wait("sp", "dve")
        has_halo = (t0 + TC) < T
        if has_halo:
            n_xh = sy.dma("sp", xh1[:], x_h[t0 + TC:t0 + TC + 1, :], ch="xh1")
        for ts in range(TSUB):
            xsm = xsm1 if ts % 2 == 0 else hidb[:, 0:D]
            chx = f"xsm{ts % 2}"
            sy.wait("sp", "act", sy.xsm_rd[ts % 2][0])
            sy.wait("sp", "dve", sy.xsm_rd[ts % 2][1])
            n_x = sy.dma("sp", xsm, x_h[t0 + ts * 128:t0 + (ts + 1) * 128, :], ch=chx)
            sy.dwait("act", chx, n_x)
            sy.dwait("dve", chx, n_x)
            ln_stats(xsm, D, MU(), RS())
            ln_apply(ln1[:, ts, :], xsm, MU(), RS())
            sy.xsm_rd[ts % 2] = (sy.cnt["act"], sy.cnt["dve"])
            ln_stats(ln1[:, ts, :], D, MU(), RS())
            ln_apply(ln2s[:, ts, :], ln1[:, ts, :], MU(), RS())
        if has_halo:
            sy.dwait("act", "xh1", n_xh)
            sy.dwait("dve", "xh1", n_xh)
            ln_stats(xh1[:], D, MU(1), RS(1))
            ln_apply(lh1[:], xh1[:], MU(1), RS(1))
            ln_stats(lh1[:], D, MU(1), RS(1))
            ln_apply(lh2[:], lh1[:], MU(1), RS(1))
            sy.wait("dve", "dve")
            sy.tick("dve", V.tensor_copy(lh2b[:], lh2[:]))
        ln2bf = hbf2.rearrange("p c t -> p (c t)")
        sy.wait("dve", "dve")
        sy.tick("dve", V.tensor_copy(ln2bf[:, :], ln2s.rearrange("p u d -> p (u d)")))
        ln2bf4 = hbf2.rearrange("p c t -> p (c t)").rearrange("p (u d) -> p u d", u=TSUB)
        for dch in range(DCH):
            sy.wait("pe", "dve")
            sy.wait("pe", "act")
            for ts in range(TSUB):
                sy.tick("pe", PE.transpose(P_fb[:, ts * 128:(ts + 1) * 128],
                                           ln2bf4[:, ts, dch * 128:(dch + 1) * 128], idb[:]))
            if has_halo:
                sy.tick("pe", PE.transpose(P_eb[:, 0:1], lh2b[0:1, dch * 128:(dch + 1) * 128], idb[0:1, 0:1]))
            sy.wait("act", "pe")
            sy.tick("act", Sc.activation(hbfb[:, dch, :], P_fb[:, 0:TC], AF.Copy))
            if has_halo:
                sy.tick("act", Sc.activation(hfmh[:, dch, :], P_eb[:, 0:1], AF.Copy))
        hidb_bfA = hidb.bitcast(BF16)
        A_slots = [wstg3.rearrange("p (c e) -> p c e", c=DCH),
                   wstg4.rearrange("p (c e) -> p c e", c=DCH)] + [
            hidb_bfA[:, 2048 + k * 1024:2048 + (k + 1) * 1024]
            .rearrange("p (c e) -> p c e", c=DCH) for k in range(6)]
        sy.wait("pe", "act")
        sy.wait("dve", "act")
        sy.wait("dve", "dve")
        sy.tick("dve", V.tensor_copy(xxb[:, :, 0:1], halo[:]))
        for ech in range(ECH):
            sl = ech % 8
            qw = ("sp", "act", "pool")[sl % 3]
            chw = (f"wsr{sl}" if qw != "pool" else f"pwsr{sl}")
            sy.wait(qw, "pe", sy.aring_rd[sl])
            stg4 = A_slots[sl]
            n_w = sy.dma(qw, stg4, winb_h[ech], ch=chw)
            sy.dwait("pe", chw, n_w)
            P_mm = P_a if ech % 2 == 0 else P_b
            sy.wait("pe", "act", sy.win_ev[ech % 2])
            for dch in range(DCH):
                sy.tick("pe", PE.matmul(P_mm[:, 0:TC], stg4[:, dch, :], hbfb[:, dch, :],
                                        start=(dch == 0), stop=(dch == DCH - 1)))
            if ech < DCH and has_halo:
                for dch in range(DCH):
                    sy.tick("pe", PE.matmul(P_c[:, ech:ech + 1], stg4[:, dch, :], hfmh[:, dch, :],
                                            start=(dch == 0), stop=(dch == DCH - 1)))
            sy.aring_rd[sl] = sy.cnt["pe"]
            sy.wait("act", "pe")
            if ech < DCH:
                sy.wait("act", "dve", lastxx)
                sy.tick("act", Sc.activation(xxb[:, ech, 1:TC + 1], P_mm[:, 0:TC], AF.Identity,
                                             bias=be_t[:, ech:ech + 1]))
            else:
                sy.tick("act", Sc.activation(szb[:, ech - DCH, :], P_mm[:, 0:TC], AF.Silu,
                                             bias=be_t[:, ech:ech + 1]))
            sy.win_ev[ech % 2] = sy.cnt["act"]
            if ech < DCH:
                sy.wait("dve", "act")
                sy.wait("dve", "dve")
                sy.tick("dve", V.tensor_scalar(xcb[:, ech, :], xxb[:, ech, 0:TC], convw[:, ech, 0:1], None, ALU.mult))
                sy.wait("dve", "dve")
                sy.tick("dve", V.scalar_tensor_tensor(xcb[:, ech, :], xxb[:, ech, 1:TC + 1], convw[:, ech, 1:2], xcb[:, ech, :], ALU.mult, ALU.add))
        if has_halo:
            sy.wait("act", "pe")
            for ech in range(DCH):
                sy.tick("act", Sc.activation(xxb[:, ech, TC + 1:TC + 2], P_c[:, ech:ech + 1], AF.Identity,
                                             bias=be_t[:, ech:ech + 1]))
        else:
            sy.wait("dve", "dve")
            sy.tick("dve", V.memset(xxb[:, :, TC + 1:TC + 2], 0.0))
        for dch in range(DCH):
            sy.wait("dve", "act")
            sy.wait("dve", "dve")
            sy.tick("dve", V.scalar_tensor_tensor(xcb[:, dch, :], xxb[:, dch, 2:TC + 2], convw[:, dch, 2:3], xcb[:, dch, :], ALU.mult, ALU.add))
        sy.wait("dve", "dve")
        sy.tick("dve", V.tensor_copy(halo[:], xxb[:, :, TC:TC + 1]))
        sy.wait("act", "dve")
        sy.wait("act", "pe")
        sy.tick("act", Sc.activation(xcb.rearrange("p c t -> p (c t)"), xcb.rearrange("p c t -> p (c t)"), AF.Silu))
        sy.wait("act", "act")
        sy.tick("act", Sc.activation(hbf2.rearrange("p c t -> p (c t)"), xcb.rearrange("p c t -> p (c t)"), AF.Copy))
        xcbf = hbf2
        sy.win_ev = {0: sy.cnt["act"], 1: sy.cnt["act"]}
        for ech in range(DCH):
            P_mm = P_a if ech % 2 == 0 else P_b
            sl = ech % 8
            qw = ("sp", "act", "pool")[sl % 3]
            chw = (f"wsr{sl}" if qw != "pool" else f"pwsr{sl}")
            sy.wait(qw, "pe", sy.aring_rd[sl])
            stg4 = A_slots[sl]
            n_w = sy.dma(qw, stg4, wdtb_h[ech], ch=chw)
            sy.dwait("pe", chw, n_w)
            sy.wait("pe", "act", sy.win_ev[ech % 2])
            for dch in range(DCH):
                sy.tick("pe", PE.matmul(P_mm[:, 0:TC], stg4[:, dch, :],
                                        xcbf[:, dch, :], start=(dch == 0), stop=(dch == DCH - 1)))
            sy.aring_rd[sl] = sy.cnt["pe"]
            sy.wait("act", "pe")
            sy.wait("act", "act")
            sy.tick("act", Sc.activation(dump[:, 0:TC], P_mm[:, 0:TC], AF.Exp, bias=bdt[:, ech:ech + 1]))
            sy.win_ev[ech % 2] = sy.cnt["act"]
            sy.wait("act", "act")
            sy.wait("act", "dve")
            sy.tick("act", Sc.activation(dtb[:, ech, :], dump[:, 0:TC], AF.Ln, bias=cONE[:]))
        sy.wait("pe", "dve")
        for dch in range(DCH):
            sy.tick("pe", PE.matmul(P_d[0:2 * S, 0:TC], wxT[:, dch, :], xcbf[:, dch, :],
                                    start=(dch == 0), stop=(dch == DCH - 1)))
        sy.wait("dve", "pe")
        sy.wait("dve", "dve")
        sy.tick("dve", V.tensor_copy(bct[:], P_d[0:2 * S, 0:TC]))
        scan_exp_t = {}
        scan_rd_t = {}
        pmul_t = {}
        gp_acc_t = {}
        bc_t = {}
        def emit_exp(s_, dch_):
            slot = (s_ * DCH + dch_) % 2
            prev = (s_ * DCH + dch_) - 2
            if prev >= 0:
                sy.wait("act", "dve", scan_rd_t.get((prev // DCH, prev % DCH), 0))
            scan_exp_t[(s_, dch_)] = sy.tick("act", Sc.activation(
                dA2[slot], dtb[:, dch_, :], AF.Exp, scale=Acol[:, dch_, s_:s_ + 1]))
        PBC = ((P_e, P_f), (P_g[:, 0:512], P_g[:, 512:1024]))

        def emit_bc(s_):
            if s_ == 0:
                sy.wait("pool", "dve", 10)
            sy.wait("pool", "pe", bc_t.get(s_ - 1, 0))
            sy.tick("pool", GP.affine_select(selB[:], ones32[:], [[0, 128]], ALU.is_equal, 0.0,
                                             base=-s_, channel_multiplier=1))
            sy.tick("pool", GP.affine_select(selC[:], ones32[:], [[0, 128]], ALU.is_equal, 0.0,
                                             base=-(S + s_), channel_multiplier=1))
            sy.wait("pe", "pool")
            if s_ > 1:
                sy.wait("pe", "dve", pmul_t.get((s_ - 2, DCH - 1), 0))
            pbB, pbC = PBC[s_ % 2]
            sy.tick("pe", PE.matmul(pbB[:, 0:TC], selB[:], bct[:], start=True, stop=True))
            bc_t[s_] = sy.tick("pe", PE.matmul(pbC[:, 0:TC], selC[:], bct[:], start=True, stop=True))
        bct_done = sy.cnt["dve"]
        sy.wait("pe", "dve", bct_done)
        emit_bc(0)
        emit_exp(0, 0)
        for s in range(S):
            if s + 1 < S:
                emit_bc(s + 1) if False else None
            if s + 1 < S:
                emit_bc(s + 1)
            for dch in range(DCH):
                k = s * DCH + dch
                slot = k % 2
                pslot = k % 2
                if k + 1 < S * DCH:
                    emit_exp((k + 1) // DCH, (k + 1) % DCH)
                sy.wait("dve", "act", scan_exp_t[(s, dch)])
                sy.wait("dve", "pe", bc_t[s])
                kprev = k - 2
                if kprev >= 0:
                    sy.wait("dve", "pool", gp_acc_t.get((kprev // DCH, kprev % DCH), 0))
                sy.tick("dve", V.tensor_tensor(dBb[:], dtb[:, dch, :], PBC[s % 2][0][:, 0:TC], ALU.mult))
                sy.wait("dve", "dve")
                sy.tick("dve", V.tensor_tensor_scan(hsc[:], dA2[slot], dBb[:], st_t[:, dch, s:s + 1], ALU.mult, ALU.add))
                sy.wait("dve", "dve")
                scan_rd_t[(s, dch)] = sy.cnt["dve"]
                sy.tick("dve", V.tensor_copy(st_t[:, dch, s:s + 1], hsc[:, TC - 1:TC]))
                pmul_t[(s, dch)] = sy.tick("dve", V.tensor_tensor(p2[pslot], hsc[:], PBC[s % 2][1][:, 0:TC], ALU.mult))
                _ACC_GP = True
                if _ACC_GP:
                    sy.wait("pool", "dve", pmul_t[(s, dch)])
                    if s == 0:
                        gp_acc_t[(s, dch)] = sy.tick("pool", GP.tensor_copy(yac[:, dch, :], p2[pslot]))
                    else:
                        gp_acc_t[(s, dch)] = sy.tick("pool", GP.tensor_tensor(yac[:, dch, :], yac[:, dch, :], p2[pslot], ALU.add))
                else:
                    sy.wait("dve", "dve")
                    if s == 0:
                        sy.tick("dve", V.tensor_copy(yac[:, dch, :], p2[pslot]))
                    else:
                        sy.tick("dve", V.tensor_tensor(yac[:, dch, :], yac[:, dch, :], p2[pslot], ALU.add))
                    gp_acc_t[(s, dch)] = 0
        sy.wait("dve", "pool")
        for dch in range(DCH):
            sy.wait("dve", "dve")
            sy.tick("dve", V.scalar_tensor_tensor(yac[:, dch, :], xcb[:, dch, :], Dp[:, dch:dch + 1], yac[:, dch, :], ALU.mult, ALU.add))
            sy.wait("dve", "dve")
            sy.wait("dve", "pe")
            sy.tick("dve", V.tensor_tensor(hbfb[:, dch, :], yac[:, dch, :], szb[:, dch, :], ALU.mult))
        ybf = hbfb
        P4 = (P_a, P_b, P_c, P_d)
        if c == 0:
            sy.dwait("pe", "wout0", sy.n_wout0)
            sy.dwait("pe", "wout1", sy.n_wout1)
        for half in range(2):
            wres = wstg if half == 0 else wstg2
            for dch in range(DCH):
                if dch == 0:
                    sy.wait("pe", "dve")
                    sy.wait("pe", "act")
                for ts in range(TSUB):
                    sy.tick("pe", PE.matmul(P4[ts][:, 0:512], ybf[:, dch, ts * 128:(ts + 1) * 128],
                                            wres[:, dch, :], start=(dch == 0), stop=(dch == DCH - 1)))
            sy.wait("dve", "pe")
            sy.wait("dve", "act")
            for ts in range(TSUB):
                sy.tick("dve", V.tensor_tensor(ln2s[:, ts, half * 512:(half + 1) * 512],
                                               P4[ts][:, 0:512],
                                               ln1[:, ts, half * 512:(half + 1) * 512], ALU.add))
        for ts in range(TSUB):
            xsm = xsm1 if ts % 2 == 0 else hidb[:, 0:D]
            chx = f"xsm{ts % 2}"
            sy.wait("sp", "act", sy.xsm_rd[ts % 2][0])
            sy.wait("sp", "dve", sy.xsm_rd[ts % 2][1])
            n_x = sy.dma("sp", xsm, x_h[t0 + ts * 128:t0 + (ts + 1) * 128, :], ch=chx)
            ln_stats(ln2s[:, ts, :], D, MU(), RS())
            ln_apply(dump[:, 0:D], ln2s[:, ts, :], MU(), RS())
            sy.dwait("dve", chx, n_x)
            sy.wait("dve", "dve")
            sy.tick("dve", V.tensor_tensor(ln1[:, ts, :], xsm, dump[:, 0:D], ALU.add))
            sy.xsm_rd[ts % 2] = (sy.cnt["act"], sy.cnt["dve"])
        sy.wait("pool", "dve")
        sy.dma("pool", x2_h[t0:t0 + TC, :].rearrange("(u p) d -> p u d", p=128), ln1[:])
        lastxx = sy.cnt["dve"]

    for e in ("dve", "act", "pe", "pool"):
        sy.dwait(e, "pool")
        sy.dwait(e, "sp")
        sy.dwait(e, "wst")
    quant.a_s = hidb
    quant.s_s = szb.rearrange("p c t -> p (c t)")
    quant.n_s = ln1.rearrange("p u d -> p (u d)")
    quant.q2s = ln2s.rearrange("p u d -> p (u d)")

    cc_finish()
    xq_tok = hbfb.rearrange("p c t -> p (c t)")
    xqv = hbfb.rearrange("p c t -> p (c t)").rearrange("p (u d) -> p u d", u=TSUB)
    xq_fm = hbf2
    hq_tok = ln2s.rearrange("p u d -> p (u d)").bitcast(BF16)[:, 0:I]
    ln2f = ln2s.rearrange("p u d -> p (u d)")
    xcbf = xcb.rearrange("p c t -> p (c t)")
    szbf = szb.rearrange("p c t -> p (c t)")
    yacf = yac.rearrange("p c t -> p (c t)")
    dtbf = dtb.rearrange("p c t -> p (c t)")
    PGU = (P_a, P_b, P_c, P_d, P_e, P_f, P_g[:, 0:512], P_g[:, 512:1024])
    PD = (P_a, P_b, P_c, P_d, P_e, P_f, P_g[:, 0:512], P_g[:, 512:1024])

    def ln4_xquant(c):
        t0 = c * TC
        for ts in range(TSUB):
            par_ = ts % 2
            xsm = xsm1 if par_ == 0 else hidb[:, 0:D]
            chx = f"xsm{par_}"
            sy.wait("sp", "act", sy.xsm_rd[par_][0])
            sy.wait("sp", "dve", sy.xsm_rd[par_][1])
            sy.dwait("sp", f"outp{par_}", sy.xsm_out[par_])
            if par_ == 1:
                sy.wait("sp", "pool")
            n_x = sy.dma("sp", xsm, x2_h[t0 + ts * 128:t0 + (ts + 1) * 128, :], ch=chx)
            sy.dwait("act", chx, n_x)
            sy.dwait("dve", chx, n_x)
            if ts == 0:
                sy.wait("dve", "pool")
                sy.wait("dve", "act")
            ln_stats(xsm, D, MU(), RS())
            ln_apply(ln1[:, ts, :], xsm, MU(), RS())
            sy.xsm_rd[par_] = (sy.cnt["act"], sy.cnt["dve"])
        sy.wait("pool", "pe", sy.hq_pe_all)
        sy.wait("dve", "pe", sy.hq_pe_all)
        sy.wait("act", "pe", sy.hq_pe_all)
        sts = []
        for ts in range(TSUB):
            bufs = (hidb[:, ts * D:(ts + 1) * D], ln2f[:, ts * D:(ts + 1) * D],
                    xcbf[:, ts * D:(ts + 1) * D], szbf[:, ts * D:(ts + 1) * D])
            sts.append((quant_s1(ln1[:, ts, :], D // 16, sin_t[:, 0:1], 578 * ts, bufs), ts))
        sy.xq_pend = sts

    def xquant_s2():
        for st, tsp in sy.xq_pend:
            quant_s2(st, xq_tok[:, tsp * D:(tsp + 1) * D])
        sy.xq_pend = []

    def xq_transpose():
        for dch in range(DCH):
            sy.wait("pe", "dve")
            sy.wait("pe", "act")
            sy.wait("pe", "pool", quant.out_t)
            for ts in range(TSUB):
                sy.tick("pe", PE.transpose(P_fb[:, ts * 128:(ts + 1) * 128],
                                           xqv[:, ts, dch * 128:(dch + 1) * 128], idb[:]))
            sy.wait("act", "pe")
            sy.tick("act", Sc.activation(xq_fm[:, dch, :], P_fb[:, 0:TC], AF.Copy))

    sy.xq_pend = []
    if "B" in phases:
        for sl in range(8):
            for chn in (f"wsr{sl}", f"pwsr{sl}"):
                sy.dwait("pool", chn)
                sy.dwait("dve", chn)
        ln4_xquant(0)
        xquant_s2()
        xq_transpose()
    for c in (range(NCH) if "B" in phases else []):
        t0 = c * TC
        sy.wait("pe", "act")

        def emit_hq_tr(qbp, rotp, hqslp):
            sy.wait("pe", "pool", sy.hid_q[rotp])
            sy.wait("pe", "act", sy.hq_ev[0])
            hq2p = hqslp.rearrange("p (u i) -> p u i", u=TSUB)
            for ts in range(TSUB):
                for bi in range(2):
                    PB = P_eb if bi == 0 else P_fb
                    sy.tick("pe", PE.transpose(PB[:, 512 + ts * 128:512 + (ts + 1) * 128],
                                               hq2p[:, ts, bi * 128:(bi + 1) * 128], idb[:]))
            sy.hq_pe[rotp] = sy.cnt["pe"]
            sy.wait("act", "pe")
            for ts in range(TSUB):
                for bi in range(2):
                    PB = P_eb if bi == 0 else P_fb
                    sy.tick("act", Sc.activation(hqT[:, qbp * 2 + bi, ts * 128:(ts + 1) * 128],
                                                 PB[:, 512 + ts * 128:512 + (ts + 1) * 128], AF.Copy))
            sy.hq_ev = (sy.cnt["act"], sy.cnt["dve"])

        pend_tr = []
        pend_q = []
        for qb in range(I // 256):
            if len(pend_tr) >= 2:
                emit_hq_tr(*pend_tr.pop(0))
            par = qb % 2
            rot = qb % 4
            stg_g = wstg[:, :, par * 256:par * 256 + 256]
            stg_u = wstg2[:, :, par * 256:par * 256 + 256]
            chg, chu = f"wg{par}", f"wu{par}"
            qg, qu = ("sp", "act") if par == 0 else ("act", "sp")
            eg, eu = ("sp", "act") if par == 0 else ("act", "sp")
            sy.wait(eg, "pe", sy.gustrip_rd[par])
            sy.wait(eu, "pe", sy.gustrip_rd[par])
            if c == 0 and qb < 2:
                sy.dwait(eg, "wqst", sy.w2_st)
                sy.dwait(eu, "wqst", sy.w2_st)
                sy.wait(eg, "pe")
                sy.wait(eu, "pe")
            n_w = sy.dma(qg, stg_g, wgT_h[qb], ch=chg)
            n_w2 = sy.dma(qu, stg_u, wuT_h[qb], ch=chu)
            sy.dwait("pe", chg, n_w)
            sy.dwait("pe", chu, n_w2)
            sy.wait("pe", "act", sy.gups_ev[par][0])
            sy.wait("pe", "dve", sy.gups_ev[par][1])
            for ts in range(TSUB):
                for dch in range(DCH):
                    sy.tick("pe", PE.matmul(PGU[ts][:, 0:256], xq_fm[:, dch, ts * 128:(ts + 1) * 128],
                                            stg_g[:, dch, :], start=(dch == 0), stop=(dch == DCH - 1)))
                for dch in range(DCH):
                    sy.tick("pe", PE.matmul(PGU[4 + ts][:, 0:256], xq_fm[:, dch, ts * 128:(ts + 1) * 128],
                                            stg_u[:, dch, :], start=(dch == 0), stop=(dch == DCH - 1)))
            sy.gustrip_rd[par] = sy.cnt["pe"]
            hsl = hidb[:, rot * 1024:(rot + 1) * 1024]
            hqsl = hq_tok[:, rot * 1024:(rot + 1) * 1024]
            sy.wait("act", "pe")
            sy.wait("act", "pool", sy.hid_q[rot])
            if qb == 0:
                sy.wait("act", "dve")
            for ts in range(TSUB):
                sy.tick("act", Sc.activation(dump[:, ts * 256:ts * 256 + 256], PGU[ts][:, 0:256], AF.Silu))
            sy.wait("dve", "act")
            sy.wait("dve", "pe")
            sy.wait("dve", "pool", sy.hid_q[rot])
            for ts in range(TSUB):
                sy.tick("dve", V.tensor_tensor(hsl[ :, ts * 256:(ts + 1) * 256], dump[:, ts * 256:ts * 256 + 256],
                                               PGU[4 + ts][:, 0:256], ALU.mult))
            sy.gups_ev[par] = (sy.cnt["act"], sy.cnt["dve"])
            sy.wait("pool", "pe", sy.hq_pe[rot])
            bufs = (xcbf[:, rot * 1024:(rot + 1) * 1024], szbf[:, rot * 1024:(rot + 1) * 1024],
                    yacf[:, rot * 1024:(rot + 1) * 1024], dtbf[:, rot * 1024:(rot + 1) * 1024])
            st = quant_s1(hsl, TSUB * 16, sind_t[:, 0:1], 578 * rot, bufs)
            pend_q.append((st, qb, rot, hqsl))
            if len(pend_q) >= 2:
                stp, qbp, rotp, hqslp = pend_q.pop(0)
                sy.hid_q[rotp] = quant_s2(stp, hqslp)
                pend_tr.append((qbp, rotp, hqslp))
        for stp, qbp, rotp, hqslp in pend_q:
            sy.hid_q[rotp] = quant_s2(stp, hqslp)
            pend_tr.append((qbp, rotp, hqslp))
        for t_ in pend_tr:
            emit_hq_tr(*t_)
        sy.hq_pe_all = sy.cnt["pe"]
        if c + 1 < NCH:
            ln4_xquant(c + 1)
        wstgf = wstg.rearrange("p c t -> p (c t)")
        wstg2f = wstg2.rearrange("p c t -> p (c t)")
        D_slots = [wstg3[:, 0:1024], wstg4[:, 0:1024]] + [
            wstgf[:, k * 1024:(k + 1) * 1024] for k in range(4)] + [
            wstg2f[:, k * 1024:(k + 1) * 1024] for k in range(2)]
        pe_dbase = sy.cnt["pe"]
        for sl in range(8):
            sy.dring_rd[sl] = max(sy.dring_rd[sl], pe_dbase)
        for ich in range(ICH):
            sl = ich % 8
            qd = ("sp", "act", "pool")[sl % 3]
            chw = (f"dsr{sl}" if qd != "pool" else f"pdsr{sl}")
            sy.wait(qd, "pe", sy.dring_rd[sl])
            stg = D_slots[sl]
            n_w = sy.dma(qd, stg,
                         wdT_h[ich * 128:(ich + 1) * 128, :], ch=chw)
            sy.dwait("pe", chw, n_w)
            if ich == 0:
                sy.wait("pe", "dve")
                sy.wait("pe", "act")
            for ts in range(TSUB):
                for hf in range(2):
                    sy.tick("pe", PE.matmul(PD[ts * 2 + hf][:, 0:512], hqT[:, ich, ts * 128:(ts + 1) * 128],
                                            stg[:, hf * 512:(hf + 1) * 512],
                                            start=(ich == 0), stop=(ich == ICH - 1)))
            sy.dring_rd[sl] = sy.cnt["pe"]
        if c + 1 < NCH:
            xquant_s2()
        sy.gustrip_rd = {0: sy.cnt["pe"], 1: sy.cnt["pe"]}
        for ts in range(TSUB):
            for hf in range(2):
                sy.wait("dve", "pe")
                sy.wait("dve", "act")
                sy.tick("dve", V.tensor_copy(dtbf.rearrange("p (u d) -> p u d", u=TSUB)[:, ts, hf * 512:(hf + 1) * 512], PD[ts * 2 + hf][:, 0:512]))
        if c + 1 < NCH:
            xq_transpose()
        dnout = dtbf.rearrange("p (u d) -> p u d", u=TSUB)
        for ts in range(TSUB):
            par_ = ts % 2
            xsm = xsm1 if par_ == 0 else hidb[:, 0:D]
            chx = f"xsm{par_}"
            sy.wait("sp", "act", sy.xsm_rd[par_][0])
            sy.wait("sp", "dve", sy.xsm_rd[par_][1])
            sy.dwait("sp", f"outp{par_}", sy.xsm_out[par_])
            if par_ == 1:
                sy.wait("sp", "pool")
            n_x = sy.dma("sp", xsm, x2_h[t0 + ts * 128:t0 + (ts + 1) * 128, :], ch=chx)
            ln_stats(dnout[:, ts, :], D, MU(), RS())
            ln_apply(dump[:, 0:D], dnout[:, ts, :], MU(), RS())
            sy.dwait("dve", chx, n_x)
            sy.wait("dve", "dve")
            sy.tick("dve", V.tensor_tensor(xsm, xsm, dump[:, 0:D], ALU.add))
            sy.wait("sp", "dve")
            sy.xsm_out[par_] = sy.dma("sp", out_h[t0 + ts * 128:t0 + (ts + 1) * 128, :], xsm, ch=f"outp{par_}")
            sy.xsm_rd[par_] = (sy.cnt["act"], sy.cnt["dve"])
    if "B" not in phases:
        sy.wait("sp", "dve")
        sy.wait("sp", "act")
        for c in range(NCH):
            sy.dma("sp", out_h[c * TC:(c + 1) * TC, :].rearrange("(u p) d -> p u d", p=128), ln1[:], ch="outp")
    sy.dwait("pool", "outp0")
    sy.dwait("pool", "outp1")
    sy.dwait("pool", "outp")
    sy.dwait("pool", "pool")
    return nc




_NC_CACHE = {}

def _build_cached():
    if "nc" not in _NC_CACHE:
        _NC_CACHE["nc"] = build(T=2048, TC=512)
    return _NC_CACHE["nc"]


def kernel(L0_attn_x, L0_attn_gamma, L0_attn_beta, L0_attn_W_in, L0_attn_conv_w,
           L0_attn_W_x, L0_attn_W_dt, L0_attn_b_dt, L0_attn_A_log, L0_attn_D_param,
           L0_attn_W_out, L0_ffn_gate_w, L0_ffn_up_w, L0_ffn_down_w, L0_ffn_s_in,
           L0_ffn_s_in_down):
    import ml_dtypes
    from concourse.bass_utils import run_bass_kernel_spmd
    nc = _build_cached()

    def fm(v):
        return np.ascontiguousarray(np.asarray(v, np.float32).reshape(8, 128).T)

    f32 = lambda a: np.ascontiguousarray(np.asarray(a, np.float32))
    shared = {
        "winT_h": f32(np.asarray(L0_attn_W_in).T),
        "wdtT_h": f32(np.asarray(L0_attn_W_dt).T),
        "wxT_h": f32(np.asarray(L0_attn_W_x).T),
        "woutT_h": f32(np.asarray(L0_attn_W_out).T),
        "convw_h": np.ascontiguousarray(
            np.asarray(L0_attn_conv_w, np.float32)[:, 0, :].reshape(8, 128, 3).transpose(1, 0, 2)),
        "gamma_h": fm(L0_attn_gamma),
        "beta_h": fm(L0_attn_beta),
        "bdt_h": fm(L0_attn_b_dt),
        "Dp_h": fm(L0_attn_D_param),
        "Alog_h": np.ascontiguousarray(
            np.asarray(L0_attn_A_log, np.float32).reshape(8, 128, 16).transpose(1, 0, 2)),

        "sin_h": np.full((128, 1), np.float32(np.asarray(L0_ffn_s_in).reshape(-1)[0]), np.float32),
        "sind_h": np.full((128, 1), np.float32(np.asarray(L0_ffn_s_in_down).reshape(-1)[0]), np.float32),
        "idb_h": np.eye(128, dtype=ml_dtypes.bfloat16),
        "idf_h": np.eye(128, dtype=np.float32),
    }
    x = np.asarray(L0_attn_x, np.float32)
    gw = np.asarray(L0_ffn_gate_w, np.float32)
    uw = np.asarray(L0_ffn_up_w, np.float32)
    dw = np.asarray(L0_ffn_down_w, np.float32)
    I8, D8 = 4096 // 8, 1024 // 8
    in_maps = [dict(shared, x_h=np.ascontiguousarray(x[b]),
                    gwsh_h=np.ascontiguousarray(gw[b * I8:(b + 1) * I8]),
                    uwsh_h=np.ascontiguousarray(uw[b * I8:(b + 1) * I8]),
                    dwsh_h=np.ascontiguousarray(dw[b * D8:(b + 1) * D8]))
               for b in range(8)]
    res = run_bass_kernel_spmd(nc, in_maps, core_ids=list(range(8)))
    return np.stack([np.asarray(r["out_h"], np.float32) for r in res.results], axis=0)
